# version 22
# baseline (speedup 1.0000x reference)
import sys
if "/opt/trn_rl_repo" not in sys.path:
    sys.path.insert(0, "/opt/trn_rl_repo")

from contextlib import ExitStack

import numpy as np
import ml_dtypes
import concourse.bass as bass
import concourse.mybir as mybir
from concourse.bass_utils import run_bass_kernel_spmd
from bass_rust import IndirectOffsetOnAxis

F32 = mybir.dt.float32
BF16 = mybir.dt.bfloat16
U32 = mybir.dt.uint32
AF = mybir.ActivationFunctionType
ALU = mybir.AluOpType
AXX = mybir.AxisListType.X

NCORES = 8
BIG = 65536.0
EPS = 1e-5

FULL = dict(D=1024, H=4096, E=8, B=4, TSEQ=8192, CAPF=1.25)


def derive(cfg):
    c = dict(cfg)
    c["TT"] = c["B"] * c["TSEQ"]
    c["N"] = c["TT"] // NCORES
    c["CAP"] = int(c["TT"] / c["E"] * c["CAPF"])
    assert c["CAP"] % 128 == 0 and (c["E"] * c["CAP"] // 128) % NCORES == 0
    c["DT"] = c["D"] // 128
    c["HT"] = c["H"] // 128
    c["NT"] = c["N"] // 128
    c["TCH"] = min(512, c["N"])
    c["NCH"] = c["N"] // c["TCH"]
    c["SCH"] = min(640, c["CAP"])
    assert c["CAP"] % c["SCH"] == 0 and c["SCH"] % 128 == 0
    c["NSC"] = c["CAP"] // c["SCH"]
    c["HB"] = min(512, c["H"])
    assert c["H"] % c["HB"] == 0
    c["GPC"] = c["CAP"] // 128
    c["DHS"] = min(512, c["D"])
    c["NDV"] = c["D"] // c["DHS"]
    return c


def split_sv(sch):
    out, v = [], 0
    while v < sch:
        r = sch - v
        s = 512 if r >= 512 else r
        if r == 640:
            s = 384
        out.append((v, s))
        v += s
    return out


ENGS = ("sync", "tensor", "vector", "scalar", "gpsimd")


class Bld:
    def __init__(self, nc, es):
        self.nc = nc
        self.q = {e: [] for e in ENGS}
        self.sems = {}
        self.cnt = {}
        self.es = es

    def sem(self, name):
        if name not in self.sems:
            self.sems[name] = self.es.enter_context(self.nc.semaphore(name))
            self.cnt[name] = 0
        return self.sems[name]

    def dma(self, eng, ch, out, in_, **kw):
        h = self.sem(ch)
        self.cnt[ch] += 16
        t = self.cnt[ch]

        def f(e, out=out, in_=in_, h=h, kw=kw):
            e.dma_start(out=out, in_=in_, **kw).then_inc(h, 16)
        self.q[eng].append(f)
        return (ch, t)

    def idma(self, ch, out, out_offset, in_, in_offset, bounds):
        h = self.sem(ch)
        self.cnt[ch] += 16
        t = self.cnt[ch]
        if not hasattr(self, "bregs"):
            self.bregs = {}

        def f(e, out=out, oo=out_offset, in_=in_, io=in_offset, b=bounds, h=h):
            if b not in self.bregs:
                r = e.alloc_register()
                e.reg_mov(r, b)
                self.bregs[b] = r
            e.indirect_dma_start(out=out, out_offset=oo, in_=in_, in_offset=io,
                                 bounds_check=self.bregs[b],
                                 oob_is_err=False).then_inc(h, 16)
        self.q["gpsimd"].append(f)
        return (ch, t)

    def op(self, eng, fn):
        ch = {"tensor": "cP", "vector": "cV", "scalar": "cA", "gpsimd": "cG"}[eng]
        h = self.sem(ch)
        self.cnt[ch] += 1
        t = self.cnt[ch]
        chain = eng != "tensor"

        def f(e, fn=fn, h=h, t=t, chain=chain):
            if chain and t > 1:
                e.wait_ge(h, t - 1)
            fn(e).then_inc(h, 1)
        self.q[eng].append(f)
        return (ch, t)

    def op0(self, eng, fn):
        self.op(eng, fn)

    def wait(self, eng, ticket):
        if not ticket:
            return
        ch, t = ticket
        h = self.sem(ch)

        def f(e, h=h, t=t):
            e.wait_ge(h, t)
        self.q[eng].append(f)

    def waits(self, eng, tickets):
        best = {}

        def add(tkt):
            if not tkt:
                return
            if isinstance(tkt, list):
                for t2 in tkt:
                    add(t2)
                return
            ch, t = tkt
            if ch not in best or best[ch] < t:
                best[ch] = t
        add(list(tickets))
        for ch, t in best.items():
            self.wait(eng, (ch, t))

    def collective(self, kind, op, ins, outs, name, groups):
        h = self.sem(name)
        self.cnt[name] += 1
        t = self.cnt[name]

        def f(e, kind=kind, op=op, ins=ins, outs=outs, h=h, groups=groups):
            e.collective_compute(kind, op, replica_groups=groups,
                                 ins=ins, outs=outs).then_inc(h, 1)
        self.q["gpsimd"].append(f)
        return (name, t)

    def barrier(self):
        h = self.sem("bar")
        self.cnt["bar"] += len(ENGS)
        t = self.cnt["bar"]
        for eng in ENGS:
            def f(e, h=h, t=t):
                e.sem_inc(h, 1)
                e.wait_ge(h, t)
            self.q[eng].append(f)


def build_program(cfg, phases="1A2"):
    c = derive(cfg)
    D, H, E, CAP = c["D"], c["H"], c["E"], c["CAP"]
    DT, HT, NT, N, TT = c["DT"], c["HT"], c["NT"], c["N"], c["TT"]
    TCH, NCH, SCH, NSC, HB, GPC = (c["TCH"], c["NCH"], c["SCH"], c["NSC"],
                                   c["HB"], c["GPC"])
    DHS, NDV = c["DHS"], c["NDV"]
    FT = DT
    SV = split_sv(SCH)
    G8 = [list(range(NCORES))]
    GP = [[i, i + 1] for i in range(0, NCORES, 2)]
    GPT = SCH // 128
    JT = HB // 128
    NHB = H // HB
    HCAP = (3 * CAP // 4) // 128 * 128
    S_HALF = (HCAP + SCH - 1) // SCH

    nc = bass.Bass("TRN2", target_bir_lowering=False)
    root = ExitStack()
    bld = Bld(nc, root)
    tk = {}

    inp = lambda name, shape, dt=F32: nc.dram_tensor(name, shape, dt,
                                                     kind="ExternalInput")
    x_in = inp("x", [N, D])
    winT = inp("winT", [D, D]); walT = inp("walT", [D, D])
    wbeT = inp("wbeT", [D, D]); wvT = inp("wvT", [D, D]); woT = inp("woT", [D, D])
    b_al = inp("b_al", [128, FT]); b_be = inp("b_be", [128, FT])
    b_vv = inp("b_vv", [128, FT])
    wg = inp("wg", [D, E]); bg = inp("bg", [1, E])
    w1 = inp("w1", [D, H]); w2 = inp("w2", [D, H]); w3 = inp("w3", [H, D])
    idb = inp("idb", [128, 128], BF16)
    idf = inp("idf", [128, 128])
    u128 = inp("u128", [128, 128], BF16)
    ones1 = inp("ones1", [1, 128])
    onescol = inp("onescol", [128, 1], BF16)
    selc = inp("selc", [E, 1])
    isodd = inp("isodd", [128, 1])
    tokids = inp("tokids", [128, NT])
    ecap8 = inp("ecap8", [128, E])
    ecapb = inp("ecapb", [128, E])
    eoff8 = inp("eoff8", [128, E])
    y_out = nc.dram_tensor("y", [N, D], F32, kind="ExternalOutput")

    scr = lambda name, shape, dt=F32: nc.dram_tensor(name, shape, dt)
    shr = lambda name, shape, dt=F32: nc.dram_tensor(name, shape, dt,
                                                     addr_space="Shared")
    w1b = scr("w1b", [D, H], BF16)
    w2b = scr("w2b", [D, H], BF16)
    w3b = scr("w3b", [H, D], BF16)
    lnx_loc = scr("lnx_loc", [N, D], BF16)
    ag_lnx = shr("ag_lnx", [TT, D], BF16)
    cnt_loc = scr("cnt_loc", [1, E])
    cnt_all = shr("cnt_all", [NCORES, E])
    slot_loc = scr("slot_loc", [E * CAP // 128, 128])
    slot_sh = scr("slot_sh", [GPC, 128])
    out_loc = scr("out_loc", [CAP + 1, D], BF16)
    out_all = shr("out_all", [E * (CAP + 1), D], BF16)
    xr1 = scr("xr1", [N, D])
    xr2 = scr("xr2", [N, D])
    alpha_sp = scr("alpha_sp", [FT * 128, N])
    bvv_sp = scr("bvv_sp", [FT * 128, N])
    hend_loc = scr("hend_loc", [128, FT])
    hend_all = scr("hend_all", [2 * 128, FT])

    def SB(stack, name, shape, dt=F32):
        return stack.enter_context(nc.sbuf_tensor(name, shape, dt))

    idb_s = SB(root, "idb_s", [128, 128], BF16)
    idf_s = SB(root, "idf_s", [128, 128])
    u128_s = SB(root, "u128_s", [128, 128], BF16)
    ones1_s = SB(root, "ones1_s", [1, 128])
    onescol_s = SB(root, "onescol_s", [128, 1], BF16)
    selc_s = SB(root, "selc_s", [E, 1])
    isodd_s = SB(root, "isodd_s", [128, 1])
    tokids_s = SB(root, "tokids_s", [128, NT])
    ecap8_s = SB(root, "ecap8_s", [128, E])
    ecapb_s = SB(root, "ecapb_s", [128, E])
    eoff8_s = SB(root, "eoff8_s", [128, E])
    bal_s = SB(root, "bal_s", [128, FT])
    bbe_s = SB(root, "bbe_s", [128, FT])
    bvv_s = SB(root, "bvv_s", [128, FT])
    wg_s = SB(root, "wg_s", [128, DT * E], BF16)
    bg_s = SB(root, "bg_s", [1, E])
    bgt_s = SB(root, "bgt_s", [128, E])
    zeros_s = SB(root, "zeros_s", [128, E * CAP // 128])
    NXB = 3
    xt_b = [SB(root, f"xt{i}", [128, D]) for i in range(NXB)]
    sq_b = [SB(root, f"sq{i}", [128, D]) for i in range(1)]
    red_b = [SB(root, f"red{i}", [128, 1]) for i in range(2)]
    ss_b = [SB(root, f"ss{i}", [128, 1]) for i in range(2)]
    sd_b = [SB(root, f"sd{i}", [128, 1]) for i in range(2)]
    mneg_b = [SB(root, f"mneg{i}", [128, 1]) for i in range(2)]
    rstd_b = [SB(root, f"rstd{i}", [128, 1]) for i in range(2)]
    ln_b = [SB(root, f"ln{i}", [128, D], BF16) for i in range(2)]
    lnT_b = [SB(root, f"lnT{i}", [128, DT * 128], BF16) for i in range(2)]
    STGW = min(2048, max(H, D))
    stg_b = [SB(root, f"stg{i}", [128, STGW], BF16) for i in range(2)]
    eps_s = SB(root, "eps_s", [128, 1])

    psGC = root.enter_context(nc.psum_tensor("psGC", [128, 512], F32))
    psT = root.enter_context(nc.psum_tensor("psT", [128, DT * 128], BF16))
    pool = [root.enter_context(nc.psum_tensor(f"pool{i}", [128, 512], F32))
            for i in range(6)]

    CUM0 = 256

    def emit_setup():
        pairs = [(idb_s, idb), (idf_s, idf), (u128_s, u128), (ones1_s, ones1),
                 (onescol_s, onescol), (selc_s, selc), (isodd_s, isodd),
                 (tokids_s, tokids), (ecap8_s, ecap8), (ecapb_s, ecapb),
                 (eoff8_s, eoff8),
                 (bal_s, b_al),
                 (bbe_s, b_be), (bvv_s, b_vv), (bg_s, bg)]
        for s, d in pairs:
            tk["consts"] = bld.dma("sync", "qCN", s[:, :], d.ap())
        for a in range(DT):
            tk["wg"] = bld.dma("gpsimd", "qWG", wg_s[:, a * E:(a + 1) * E],
                               wg.ap()[a * 128:(a + 1) * 128, :])
        bld.waits("tensor", [tk["consts"], tk["wg"]])
        t_bm = bld.op("tensor", lambda e: e.matmul(
            psGC[:, 0:E], ones1_s[0:1, :], bg_s[0:1, :], start=True, stop=True))
        bld.wait("vector", t_bm)
        tk["bgt"] = bld.op("vector", lambda e: e.tensor_copy(bgt_s[:, :],
                                                             psGC[:, 0:E]))
        bld.op0("vector", lambda e: e.memset(eps_s[:, :], EPS))
        bld.op0("vector", lambda e: e.memset(stg_b[0][0:1, 0:D], 0.0))
        tk["zeros"] = bld.op("vector", lambda e: e.memset(zeros_s[:, :], 0.0))
        bld.wait("vector", tk["zeros"])
        bld.wait("scalar", tk["zeros"])
        bld.wait("gpsimd", tk["zeros"])
        tk["zrow"] = bld.dma("gpsimd", "qZR", out_loc.ap()[CAP:CAP + 1, :],
                             stg_b[0][0:1, 0:D])

    def emit_precast(part):
        cons = tk.get("pc_cons", [tk.get("zrow"), None])
        j = tk.get("pc_j", 0)

        def pair(src, dst, a, c0, cw):
            nonlocal j
            b = j % 2
            bld.waits("gpsimd", [cons[b]])
            tin = bld.dma("gpsimd", f"qSTG{b}", stg_b[b][:, 0:cw],
                          src.ap()[a * 128:(a + 1) * 128, c0:c0 + cw])
            bld.wait("scalar", tin)
            cons[b] = bld.dma(
                "scalar", f"qSTS{b}",
                dst.ap()[a * 128:(a + 1) * 128, c0:c0 + cw],
                stg_b[b][:, 0:cw])
            j += 1
            return cons[b]

        blocks = ([0] if part in ("A",) else
                  list(range(1, (H + STGW - 1) // STGW)) if part == "B" else [])
        for blk_i in blocks:
            c0 = blk_i * STGW
            cw = min(STGW, H - c0)
            for a in range(DT):
                pair(w1, w1b, a, c0, cw)
            for a in range(DT):
                pair(w2, w2b, a, c0, cw)
            tk[("wfc", blk_i)] = [cons[0], cons[1]]
        if part == "C":
            for a in range(HT):
                tk[("w3r", a)] = pair(w3, w3b, a, 0, D)
        tk["wffn"] = [cons[0], cons[1]]
        tk["pc_cons"] = cons
        tk["pc_j"] = j

    def emit_ln_stats(i, x_ap, lnb, rb, dep=None, alt=False):
        xb = xt_b[i % NXB]
        r = i % 2
        lr = i % rb
        veng = "gpsimd" if alt else "vector"
        bld.waits("sync", [tk.get(("xt_free", i - NXB)), dep])
        tx = bld.dma("sync", f"qX{i % NXB}", xb[:, :], x_ap[i * 128:(i + 1) * 128, :])
        bld.waits("scalar", [tx, tk.get(("stats_free", i - 2))])
        t_red = bld.op("scalar", lambda e, xb=xb, r=r: e.activation(
            sq_b[0][:, :], xb[:, :], AF.Identity, accum_out=red_b[r][:, 0:1]))
        bld.waits(veng, [t_red])
        t_mn = bld.op(veng, lambda e, r=r: e.tensor_scalar(
            mneg_b[r][:, :], red_b[r][:, :], -1.0 / D, None, ALU.mult))
        bld.waits("scalar", [t_mn])
        t_sq = bld.op("scalar", lambda e, xb=xb, r=r: e.activation(
            sq_b[0][:, :], xb[:, :], AF.Square, bias=mneg_b[r][:, 0:1],
            scale=1.0, accum_out=ss_b[r][:, 0:1]))
        bld.wait("scalar", t_sq)
        if alt:
            t_sd = bld.op("scalar", lambda e, r=r: e.activation(
                sd_b[r][:, :], ss_b[r][:, :], AF.Ln, bias=eps_s[:, 0:1],
                scale=1.0 / D))
            bld.wait("scalar", t_sd)
            t_rs = bld.op("scalar", lambda e, r=r: e.activation(
                rstd_b[r][:, :], sd_b[r][:, :], AF.Exp, scale=-0.5))
        else:
            t_sd = bld.op("scalar", lambda e, r=r: e.activation(
                sd_b[r][:, :], ss_b[r][:, :], AF.Sqrt, bias=eps_s[:, 0:1],
                scale=1.0 / D))
            bld.wait("vector", t_sd)
            t_rs = bld.op("vector", lambda e, r=r: e.reciprocal(
                rstd_b[r][:, :], sd_b[r][:, :]))
        bld.waits(veng, [t_rs, tk.get(("ln_free", i - rb))])
        t_ln = bld.op(veng, lambda e, xb=xb, r=r, lnb=lnb, lr=lr:
                      e.tensor_scalar(
            lnb[lr][:, :], xb[:, :], mneg_b[r][:, 0:1], rstd_b[r][:, 0:1],
            ALU.add, ALU.mult))
        tk[("xt_free", i)] = t_ln
        tk[("stats_free", i)] = t_ln
        return {"ln": t_ln, "lr": lr}

    def emit_ln_tr(i, t_ln, lnb, rb, lnT_dst):
        lr = i % rb
        bld.waits("tensor", [t_ln, tk.get("psT_free")])
        for a in range(DT):
            t_tp = bld.op("tensor", lambda e, a=a, lnb=lnb, lr=lr: e.transpose(
                psT[:, a * 128:(a + 1) * 128],
                lnb[lr][:, a * 128:(a + 1) * 128], idb_s[:, :]))
        bld.wait("scalar", t_tp)
        t_ev = bld.op("scalar", lambda e, d=lnT_dst: e.copy(d, psT[:, :]))
        tk["psT_free"] = t_ev
        return t_ev

    def emit_ln_tile(i, x_ap, store_dram, lnT_dst, alt=False, dep=None):
        out = emit_ln_stats(i, x_ap, ln_b, 2, alt=alt, dep=dep)
        t_ln = out["ln"]
        out["free"] = [t_ln]
        if store_dram is not None:
            bld.wait("sync", t_ln)
            out["store"] = bld.dma("sync", f"qLS{i % 2}",
                                   store_dram[i * 128:(i + 1) * 128, :],
                                   ln_b[i % 2][:, :])
            out["free"].append(out["store"])
        if lnT_dst is not None:
            t_ev = emit_ln_tr(i, t_ln, ln_b, 2, lnT_dst)
            out["lnT"] = t_ev
            out["free"].append(t_ev)
        tk[("ln_free", i)] = out["free"]
        return out

    def emit_ffn(ph, x_ap, y_ap, keep_scope=False, dep_fn=None):
        pes = ExitStack()
        logits_s = SB(pes, f"logits_s{ph}", [128, NT * E])
        m1_s = SB(pes, f"m1_s{ph}", [128, NT])
        m2_s = SB(pes, f"m2_s{ph}", [128, NT])
        w1r_s = SB(pes, f"w1r_s{ph}", [128, NT])
        mask1_s = SB(pes, f"mask1_s{ph}", [128, NT * E], BF16)
        mask2_s = SB(pes, f"mask2_s{ph}", [128, NT * E], BF16)
        label_s = SB(pes, f"label_s{ph}", [128, NT * E], BF16)
        masked_s = SB(pes, f"masked_s{ph}", [128, NT * E])
        seq_s = SB(pes, f"seq_s{ph}", [128, NT * E])
        va_s = SB(pes, f"va_s{ph}", [128, NT * E])
        valid_s = SB(pes, f"valid_s{ph}", [128, NT * E])
        fxf_s = SB(pes, f"fxf_s{ph}", [128, NT * E])
        ffa_s = SB(pes, f"ffa_s{ph}", [128, NT * E])
        fidx1_s = SB(pes, f"fidx1_s{ph}", [128, NT])
        fidx2_s = SB(pes, f"fidx2_s{ph}", [128, NT])
        fidx1_u = SB(pes, f"fidx1_u{ph}", [128, NT], U32)
        fidx2_u = SB(pes, f"fidx2_u{ph}", [128, NT], U32)
        k1_s = SB(pes, f"k1_s{ph}", [128, NT])
        k2_s = SB(pes, f"k2_s{ph}", [128, NT])
        k1_u = SB(pes, f"k1_u{ph}", [128, NT], U32)
        k2_u = SB(pes, f"k2_u{ph}", [128, NT], U32)
        w1h_s = SB(pes, f"w1h_s{ph}", [128, NT])
        w2h_s = SB(pes, f"w2h_s{ph}", [128, NT])
        run_s = SB(pes, f"run_s{ph}", [1, 2 * E])
        cnts_s = SB(pes, f"cnts_s{ph}", [NCORES, E])
        slotsh_s = SB(pes, f"slotsh_s{ph}", [GPC, 128])
        sidxf_s = SB(pes, f"sidxf_s{ph}", [128, GPC])
        sidx_u = SB(pes, f"sidx_u{ph}", [128, GPC], U32)
        NQ = N // 4
        ln_stores = []
        t_agl = []
        for i in range(NT):
            bld.waits("scalar", [tk.get((ph, "gate", i - 2))])
            r = emit_ln_tile(i, x_ap, lnx_loc.ap(), lnT_b[i % 2][:, :],
                             alt=(ph == 2),
                             dep=dep_fn(i) if dep_fn else None)
            ln_stores.append(r["store"])
            if i == NT // 4 - 1 or i == NT // 2 - 1:
                q = (i + 1) // (NT // 4) - 1
                bld.waits("gpsimd", ln_stores)
                t_agl.append(bld.collective(
                    "AllGather", ALU.bypass,
                    [lnx_loc.ap()[q * NQ:(q + 1) * NQ, :].opt()],
                    [ag_lnx.ap()[NCORES * q * NQ:NCORES * (q + 1) * NQ,
                                 :].opt()], f"ccL{ph}{'ab'[q]}", G8))
            bld.waits("tensor", [r["lnT"]])
            if i == 0:
                bld.waits("tensor", [tk["wg"], tk["bgt"], tk["consts"]])
            for a in range(DT):
                t_g = bld.op("tensor", lambda e, a=a, i=i, rr=i % 2: e.matmul(
                    psGC[:, i * E:(i + 1) * E],
                    lnT_b[rr][:, a * 128:(a + 1) * 128],
                    wg_s[:, a * E:(a + 1) * E],
                    start=(a == 0), stop=(a == DT - 1)))
            tk[(ph, "gate", i)] = t_g
        tk[(ph, "lnst")] = ln_stores
        bld.waits("vector", [t_g])
        l3 = logits_s[:, :].rearrange("p (i e) -> p i e", e=E)
        t_logev = bld.op("vector", lambda e: e.tensor_tensor(
            l3, psGC[:, 0:NT * E].rearrange("p (i e) -> p i e", e=E),
            bgt_s[:, :].unsqueeze(1).broadcast_to([128, NT, E]), ALU.add))
        bld.op0("vector", lambda e: e.tensor_reduce(
            m1_s[:, :], l3, axis=AXX, op=ALU.max))
        m1b = m1_s[:, :].unsqueeze(2).broadcast_to([128, NT, E])
        mk1 = mask1_s[:, :].rearrange("p (i e) -> p i e", e=E)
        bld.op0("vector", lambda e: e.tensor_tensor(mk1, l3, m1b, ALU.is_equal))
        md3 = masked_s[:, :].rearrange("p (i e) -> p i e", e=E)
        bld.op0("vector", lambda e: e.scalar_tensor_tensor(
            md3, mk1, -1e30, l3, ALU.mult, ALU.add))
        bld.op0("vector", lambda e: e.tensor_reduce(
            m2_s[:, :], md3, axis=AXX, op=ALU.max))
        m2b = m2_s[:, :].unsqueeze(2).broadcast_to([128, NT, E])
        mk2 = mask2_s[:, :].rearrange("p (i e) -> p i e", e=E)
        bld.op0("vector", lambda e: e.tensor_tensor(mk2, md3, m2b, ALU.is_equal))
        lb3 = label_s[:, :].rearrange("p (i e) -> p i e", e=E)
        t_lab = bld.op("vector", lambda e: e.tensor_tensor(lb3, mk1, mk2, ALU.add))
        d12_s = masked_s
        t_d12 = bld.op("vector", lambda e: e.tensor_tensor(
            d12_s[:, 0:NT], m1_s[:, :], m2_s[:, :], ALU.subtract))
        bld.wait("scalar", t_d12)
        t_w1r = bld.op("scalar", lambda e: e.activation(
            w1r_s[:, :], d12_s[:, 0:NT], AF.Sigmoid))
        bld.wait("vector", t_w1r)
        t_wh = bld.op("vector", lambda e: e.tensor_scalar(
            w1h_s[:, :], w1r_s[:, :], 0.5, None, ALU.mult))
        bld.wait("vector", t_wh)
        t_wh2 = bld.op("vector", lambda e: e.tensor_scalar(
            w2h_s[:, :], w1h_s[:, :], 0.5, -1.0, ALU.subtract, ALU.mult))
        bld.waits("tensor", [t_lab])
        for i in range(NT):
            t_cm = bld.op("tensor", lambda e, i=i: e.matmul(
                psGC[0:1, 0:E], onescol_s[:, :], label_s[:, i * E:(i + 1) * E],
                start=(i == 0), stop=(i == NT - 1)))
        bld.wait("vector", t_cm)
        t_cev = bld.op("vector", lambda e: e.tensor_copy(
            run_s[0:1, 0:E], psGC[0:1, 0:E]))
        bld.wait("sync", t_cev)
        t_cst = bld.dma("sync", "qCT", cnt_loc.ap(), run_s[0:1, 0:E])
        bld.waits("gpsimd", [t_cst])
        t_agc = bld.collective("AllGather", ALU.bypass,
                               [cnt_loc.ap().opt()], [cnt_all.ap().opt()],
                               f"ccC{ph}", G8)
        bld.wait("gpsimd", t_agc)
        bld.waits("gpsimd", [tk.get("att_done")])
        t_cld = bld.dma("gpsimd", "qCT2", cnts_s[:, :], cnt_all.ap())
        NQ2 = N // 4
        bld.waits("gpsimd", [tk[(ph, "lnst")][:3 * NT // 4]])
        t_agl.append(bld.collective(
            "AllGather", ALU.bypass, [lnx_loc.ap()[2 * NQ2:3 * NQ2, :].opt()],
            [ag_lnx.ap()[NCORES * 2 * NQ2:NCORES * 3 * NQ2, :].opt()],
            f"ccL{ph}c", G8))
        bld.waits("tensor", [t_cld])
        t_bse = bld.op("tensor", lambda e: e.matmul(
            psGC[0:1, 8:8 + E], selc_s[:, :], cnts_s[:, :],
            start=True, stop=True))
        bld.wait("vector", t_bse)
        t_run = bld.op("vector", lambda e: e.tensor_copy(
            run_s[0:1, 0:E], psGC[0:1, 8:8 + E]))
        for i in range(NT):
            par, nxt = (i % 2) * E, ((i + 1) % 2) * E
            bld.wait("tensor", t_run)
            bld.op0("tensor", lambda e, i=i: e.matmul(
                psGC[:, CUM0 + i * E:CUM0 + (i + 1) * E], u128_s[:, :],
                label_s[:, i * E:(i + 1) * E], start=True, stop=False))
            bld.op0("tensor", lambda e, i=i, par=par: e.matmul(
                psGC[:, CUM0 + i * E:CUM0 + (i + 1) * E], ones1_s[0:1, :],
                run_s[0:1, par:par + E], start=False, stop=True))
            t_tot = bld.op("tensor", lambda e, i=i: e.matmul(
                psGC[0:1, 16:16 + E], onescol_s[:, :],
                label_s[:, i * E:(i + 1) * E], start=True, stop=True))
            bld.wait("vector", t_tot)
            t_run = bld.op("vector", lambda e, par=par, nxt=nxt: e.tensor_tensor(
                run_s[0:1, nxt:nxt + E], run_s[0:1, par:par + E],
                psGC[0:1, 16:16 + E], ALU.add))
        bld.wait("vector", t_run)
        cum3 = psGC[:, CUM0:CUM0 + NT * E].rearrange("p (i e) -> p i e", e=E)
        sq3 = seq_s[:, :].rearrange("p (i e) -> p i e", e=E)
        bld.op0("vector", lambda e: e.tensor_tensor(sq3, cum3, lb3, ALU.mult))
        bld.op0("vector", lambda e: e.tensor_scalar(
            va_s[:, :], seq_s[:, :], 1.0, None, ALU.is_ge))
        bld.op0("vector", lambda e: e.scalar_tensor_tensor(
            valid_s[:, :], seq_s[:, :], float(CAP), va_s[:, :],
            ALU.is_le, ALU.mult))
        u3 = fxf_s[:, :].rearrange("p (i e) -> p i e", e=E)
        bld.op0("vector", lambda e: e.tensor_scalar(
            ffa_s[:, :], seq_s[:, :], float(HCAP), None, ALU.is_gt))
        bld.op0("vector", lambda e: e.tensor_tensor(
            u3, sq3, ecap8_s[:, :].unsqueeze(1).broadcast_to([128, NT, E]),
            ALU.add))
        TRASH = float(E * HCAP + (CAP - HCAP))
        bld.op0("vector", lambda e: e.tensor_tensor(
            va_s[:, :].rearrange("p (i e) -> p i e", e=E),
            ffa_s[:, :].rearrange("p (i e) -> p i e", e=E),
            eoff8_s[:, :].unsqueeze(1).broadcast_to([128, NT, E]), ALU.mult))
        bld.op0("vector", lambda e: e.tensor_tensor(
            fxf_s[:, :], fxf_s[:, :], va_s[:, :], ALU.add))
        bld.op0("vector", lambda e: e.tensor_scalar(
            fxf_s[:, :], fxf_s[:, :], TRASH + 1.0, None, ALU.subtract))
        bld.op0("vector", lambda e: e.tensor_tensor(
            fxf_s[:, :], valid_s[:, :], fxf_s[:, :], ALU.mult))
        bld.op0("vector", lambda e: e.tensor_scalar(
            fxf_s[:, :], fxf_s[:, :], TRASH, None, ALU.add))
        fx3 = fxf_s[:, :].rearrange("p (i e) -> p i e", e=E)
        ff3 = ffa_s[:, :].rearrange("p (i e) -> p i e", e=E)
        bld.op0("vector", lambda e: e.tensor_tensor(ff3, fx3, mk1, ALU.mult))
        bld.op0("vector", lambda e: e.tensor_reduce(
            fidx1_s[:, :], ff3, axis=AXX, op=ALU.add))
        bld.op0("vector", lambda e: e.tensor_tensor(ff3, fx3, mk2, ALU.mult))
        t_f2r = bld.op("vector", lambda e: e.tensor_reduce(
            fidx2_s[:, :], ff3, axis=AXX, op=ALU.add))
        bld.wait("vector", t_f2r)
        bld.op0("vector", lambda e: e.tensor_copy(fidx1_u[:, :], fidx1_s[:, :]))
        bld.op0("vector", lambda e: e.tensor_copy(fidx2_u[:, :], fidx2_s[:, :]))
        mx3 = masked_s[:, :].rearrange("p (i e) -> p i e", e=E)
        bld.op0("vector", lambda e: e.tensor_tensor(
            mx3, sq3, ecapb_s[:, :].unsqueeze(1).broadcast_to([128, NT, E]),
            ALU.add))
        bld.op0("vector", lambda e: e.tensor_scalar(
            masked_s[:, :], masked_s[:, :], BIG + 1.0, None, ALU.subtract))
        bld.op0("vector", lambda e: e.tensor_tensor(
            masked_s[:, :], valid_s[:, :], masked_s[:, :], ALU.mult))
        bld.op0("vector", lambda e: e.tensor_scalar(
            masked_s[:, :], masked_s[:, :], BIG, None, ALU.add))
        bld.op0("vector", lambda e: e.tensor_tensor(ff3, mx3, mk1, ALU.mult))
        bld.op0("vector", lambda e: e.tensor_reduce(
            k1_s[:, :], ff3, axis=AXX, op=ALU.add))
        bld.op0("vector", lambda e: e.tensor_tensor(ff3, mx3, mk2, ALU.mult))
        t_k2r = bld.op("vector", lambda e: e.tensor_reduce(
            k2_s[:, :], ff3, axis=AXX, op=ALU.add))
        bld.wait("vector", t_k2r)
        bld.op0("vector", lambda e: e.tensor_copy(k1_u[:, :], k1_s[:, :]))
        t_fu = bld.op("vector", lambda e: e.tensor_copy(k2_u[:, :], k2_s[:, :]))
        bld.waits("gpsimd", [tk["zeros"]])
        t_zero = bld.dma("gpsimd", "qZ",
                         slot_loc.ap().rearrange("r c -> (r c)")
                         .rearrange("(p f) -> p f", p=128),
                         zeros_s[:, :])
        bld.waits("gpsimd", [t_fu, t_zero])
        slot_flat = slot_loc.ap().rearrange("r c -> (r c)").unsqueeze(1)
        for i in range(NT):
            for fu in (k1_u, k2_u):
                t_sc = bld.idma("qSC", slot_flat,
                                IndirectOffsetOnAxis(ap=fu[:, i:i + 1], axis=0),
                                tokids_s[:, i:i + 1], None, E * CAP - 1)
        bld.wait("gpsimd", t_sc)
        t_rsc = bld.collective("ReduceScatter", ALU.add,
                               [slot_loc.ap().opt()], [slot_sh.ap().opt()],
                               f"ccR{ph}", G8)
        bld.waits("gpsimd", [tk[(ph, "lnst")]])
        t_agl.append(bld.collective(
            "AllGather", ALU.bypass, [lnx_loc.ap()[3 * NQ:N, :].opt()],
            [ag_lnx.ap()[NCORES * 3 * NQ:TT, :].opt()], f"ccL{ph}d", G8))
        t_ag_lnx = list(t_agl)
        bld.wait("gpsimd", t_rsc)
        t_shl = bld.dma("gpsimd", "qSX", slotsh_s[:, :], slot_sh.ap())
        bld.waits("tensor", [t_shl, t_logev])
        t_shT = bld.op("tensor", lambda e: e.transpose(
            psGC[:, 32:32 + GPC], slotsh_s[:, :], idf_s[0:GPC, 0:GPC]))
        bld.wait("vector", t_shT)
        bld.op0("vector", lambda e: e.tensor_copy(sidxf_s[:, :],
                                                  psGC[:, 32:32 + GPC]))
        bld.op0("vector", lambda e: e.tensor_scalar(
            sidxf_s[:, :], sidxf_s[:, :], 1.0, 0.0, ALU.subtract, ALU.max))
        t_sx = bld.op("vector", lambda e: e.tensor_copy(sidx_u[:, :],
                                                        sidxf_s[:, :]))

        pexp = ExitStack()
        grows = [SB(pexp, f"grow{ph}_{i}", [128, D], BF16) for i in range(GPT)]
        dispT = [SB(pexp, f"dispT{ph}_{i}", [128, DT * SCH], BF16)
                 for i in range(2)]
        h_sb = SB(pexp, f"h_sb{ph}", [128, HT * SCH], BF16)
        w1s = [SB(pexp, f"w1s{ph}_{i}", [128, DT * HB], BF16) for i in range(2)]
        w2s = [SB(pexp, f"w2s{ph}_{i}", [128, DT * HB], BF16) for i in range(2)]
        NW3B = 6
        w3s = [SB(pexp, f"w3s{ph}_{i}", [128, D], BF16) for i in range(NW3B)]
        s1_b = [SB(pexp, f"s1b{ph}_{i}", [128, SCH], BF16) for i in range(2)]
        outs_b = [SB(pexp, f"outs{ph}_{i}", [128, D], BF16) for i in range(2)]

        def emit_gathers(s):
            bld.waits("gpsimd", [t_sx, t_ag_lnx, tk.get((ph, "dtev", s - 1))])
            for g in range(GPT):
                t_ga = bld.idma(
                    f"qD{s % 2}",
                    grows[g][:, :], None, ag_lnx.ap(),
                    IndirectOffsetOnAxis(
                        ap=sidx_u[:, s * GPT + g:s * GPT + g + 1], axis=0),
                    TT - 1)
            tk[(ph, "tga", s)] = t_ga

        def emit_disp_tr(s):
            par = s % 2
            bld.waits("tensor", [tk[(ph, "tga", s)],
                                 tk.get((ph, "hdone", s - 2))])
            for g in range(GPT):
                bld.waits("tensor", [tk.get("psT_free")])
                for a in range(DT):
                    t_tp = bld.op("tensor", lambda e, g=g, a=a: e.transpose(
                        psT[:, a * 128:(a + 1) * 128],
                        grows[g][:, a * 128:(a + 1) * 128], idb_s[:, :]))
                bld.wait("scalar", t_tp)
                t_dev = bld.op("scalar", lambda e, g=g, par=par: e.copy(
                    dispT[par][:, :].rearrange("p (a t) -> p a t", a=DT)
                    [:, :, g * 128:(g + 1) * 128], psT[:, :]))
                tk["psT_free"] = t_dev
            tk[(ph, "dtev", s)] = t_dev

        if ph == 1:
            emit_precast("A")
        bld.waits("gpsimd", [tk.get("att_done")])
        bld.waits("sync", [tk.get("att_done")])
        emit_gathers(0)
        emit_disp_tr(0)
        if NSC > 1:
            emit_gathers(1)
        if ph == 1:
            emit_precast("B")
            emit_precast("C")

        t_ago1 = None
        for s in range(NSC):
            par = s % 2
            if s >= 1:
                emit_disp_tr(s)
                if s + 1 <= NSC - 1:
                    emit_gathers(s + 1)
            if s == S_HALF and s > 0:
                nlw = tk[(ph, "lin_w")]
                bld.waits("gpsimd", [tk.get((ph, "ost", nlw - 1)),
                                     tk.get((ph, "ost", nlw - 2))])
                t_ago1 = bld.collective(
                    "AllGather", ALU.bypass, [out_loc.ap()[0:HCAP, :].opt()],
                    [out_all.ap()[0:E * HCAP, :].opt()], f"ccO{ph}a", G8)
            t_dev = tk[(ph, "dtev", s)]
            for hb in range(NHB):
                wpar = hb % 2
                lin_hb = s * NHB + hb
                bld.waits("sync", [tk.get(("wfc", (hb * HB) // STGW)),
                                   tk.get((ph, "wcons", lin_hb - 2))])
                t_w1 = bld.dma("sync", f"qW{wpar}", w1s[wpar][:, :],
                               w1b.ap().rearrange("(a p) h -> p a h", p=128)
                               [:, :, hb * HB:(hb + 1) * HB])
                t_w2 = bld.dma("sync", f"qW{wpar}", w2s[wpar][:, :],
                               w2b.ap().rearrange("(a p) h -> p a h", p=128)
                               [:, :, hb * HB:(hb + 1) * HB])
                for j in range(JT):
                    jj = hb * JT + j
                    for (v0, sv) in SV:
                        pp = (jj * len(SV) + SV.index((v0, sv))) % 2
                        bld.waits("tensor", [t_dev, t_w1, t_w2,
                                             tk.get((ph, "h1f", pp))])
                        for a in range(DT):
                            t_h1 = bld.op("tensor", lambda e, a=a, j=j, v0=v0,
                                          sv=sv, par=par, wpar=wpar, pp=pp:
                                          e.matmul(
                                pool[pp][:, 0:sv],
                                w1s[wpar][:, a * HB + j * 128:
                                          a * HB + (j + 1) * 128],
                                dispT[par][:, a * SCH + v0:a * SCH + v0 + sv],
                                start=(a == 0), stop=(a == DT - 1)))
                        bld.waits("scalar", [t_h1, tk.get((ph, "s1f", pp))])
                        t_s1 = bld.op("scalar", lambda e, sv=sv, pp=pp:
                                      e.activation(s1_b[pp][:, 0:sv],
                                                   pool[pp][:, 0:sv], AF.Silu))
                        tk[(ph, "h1f", pp)] = t_s1
                        tk[(ph, "siluf")] = t_s1
                        bld.waits("tensor", [tk.get((ph, "h2f", pp))])
                        for a in range(DT):
                            t_h2 = bld.op("tensor", lambda e, a=a, j=j, v0=v0,
                                          sv=sv, par=par, wpar=wpar, pp=pp:
                                          e.matmul(
                                pool[2 + pp][:, 0:sv],
                                w2s[wpar][:, a * HB + j * 128:
                                          a * HB + (j + 1) * 128],
                                dispT[par][:, a * SCH + v0:a * SCH + v0 + sv],
                                start=(a == 0), stop=(a == DT - 1)))
                        bld.waits("vector", [t_s1, t_h2])
                        t_h = bld.op("vector", lambda e, jj=jj, v0=v0, sv=sv,
                                     pp=pp: e.tensor_tensor(
                            h_sb[:, jj * SCH + v0:jj * SCH + v0 + sv],
                            s1_b[pp][:, 0:sv], pool[2 + pp][:, 0:sv],
                            ALU.mult))
                        tk[(ph, "s1f", pp)] = t_h
                        tk[(ph, "h2f", pp)] = t_h
                        tk[(ph, "multf")] = t_h
                tk[(ph, "wcons", lin_hb)] = [t_h1, t_h2]
            tk[(ph, "hdone", s)] = t_h
            for vidx, (v0, sv) in enumerate(SV):
                ntile = sv // 128
                for jj in range(HT):
                    w3par = jj % NW3B
                    lin_jj = (s * len(SV) + vidx) * HT + jj
                    bld.waits("sync", [tk.get((ph, "w3cons", lin_jj - NW3B)),
                                       tk.get(("w3r", jj))])
                    t_w3 = bld.dma("sync", f"qT{w3par}", w3s[w3par][:, :],
                                   w3b.ap()[jj * 128:(jj + 1) * 128, :])
                    bld.waits("tensor", [t_w3, tk[(ph, "hdone", s)]])
                    if jj == 0:
                        bld.waits("tensor", [tk.get((ph, "siluf")),
                                             tk.get((ph, "multf")),
                                             tk.get((ph, "oevlast"))])
                    for w in range(ntile):
                        for dv in range(NDV):
                            t_o = bld.op("tensor", lambda e, jj=jj, w=w, dv=dv,
                                         v0=v0, w3par=w3par: e.matmul(
                                pool[w * NDV + dv][:, 0:DHS],
                                h_sb[:, jj * SCH + v0 + w * 128:
                                     jj * SCH + v0 + (w + 1) * 128],
                                w3s[w3par][:, dv * DHS:(dv + 1) * DHS],
                                start=(jj == 0), stop=(jj == HT - 1)))
                    tk[(ph, "w3cons", lin_jj)] = t_o
                tk[(ph, "w3mm", s)] = t_o
                bld.wait("scalar", t_o)
                for w in range(ntile):
                    lin_w = tk.get((ph, "lin_w"), 0)
                    tk[(ph, "lin_w")] = lin_w + 1
                    opar = lin_w % 2
                    bld.waits("scalar", [tk.get((ph, "ost", lin_w - 2))])
                    for dv in range(NDV):
                        t_oe = bld.op("scalar", lambda e, w=w, dv=dv, opar=opar:
                                      e.copy(outs_b[opar][:, dv * DHS:
                                                          (dv + 1) * DHS],
                                             pool[w * NDV + dv][:, 0:DHS]))
                    tk[(ph, "oevlast")] = t_oe
                    bld.wait("sync", t_oe)
                    r0 = s * SCH + v0 + w * 128
                    t_ost = bld.dma("sync", f"qO{opar}", out_loc.ap()[r0:r0 + 128, :],
                                    outs_b[opar][:, :])
                    tk[(ph, "ost", lin_w)] = t_ost
        nlw = tk[(ph, "lin_w")]
        bld.waits("gpsimd", [tk.get((ph, "ost", nlw - 1)),
                             tk.get((ph, "ost", nlw - 2))])
        if t_ago1 is None:
            t_ago1 = bld.collective(
                "AllGather", ALU.bypass, [out_loc.ap()[0:HCAP, :].opt()],
                [out_all.ap()[0:E * HCAP, :].opt()], f"ccO{ph}a", G8)
        bld.waits("gpsimd", [tk["zrow"]])
        t_ago2 = bld.collective(
            "AllGather", ALU.bypass, [out_loc.ap()[HCAP:CAP + 1, :].opt()],
            [out_all.ap()[E * HCAP:E * (CAP + 1), :].opt()], f"ccO{ph}b", G8)
        t_ago = [t_ago1, t_ago2]
        pexp.close()
        nlw2 = tk[(ph, "lin_w")]
        tk[(ph, "expfree")] = [tk.get((ph, "ost", nlw2 - 1)),
                               tk.get((ph, "ost", nlw2 - 2)),
                               tk.get((ph, "oevlast")),
                               tk[(ph, "hdone", NSC - 1)]]
        r12_b = [SB(pes, f"r12b{ph}_{i}", [128, 2 * D], BF16) for i in range(4)]
        xt2_b = [SB(pes, f"xt2{ph}_{i}", [128, D]) for i in range(4)]
        y0_b = [SB(pes, f"y0b{ph}_{i}", [128, D]) for i in range(4)]
        t_y = None
        for i in range(NT):
            p4 = i % 4
            par = i % 4
            bld.waits("vector", [tk.get((ph, "ycons", i - 4)), t_wh2])
            bld.waits("gpsimd", [tk.get((ph, "rfree", i - 4)), t_ago])
            bld.idma(f"qC{p4}", r12_b[p4][:, 0:D], None, out_all.ap(),
                     IndirectOffsetOnAxis(ap=fidx1_u[:, i:i + 1], axis=0),
                     E * (CAP + 1) - 1)
            t_gd = bld.idma(f"qC{p4}", r12_b[p4][:, D:2 * D], None, out_all.ap(),
                            IndirectOffsetOnAxis(ap=fidx2_u[:, i:i + 1], axis=0),
                            E * (CAP + 1) - 1)
            bld.waits("scalar", [tk.get((ph, "ycons", i - 4))])
            t_x = bld.dma("scalar", f"qCX{par}", xt2_b[par][:, :],
                          x_ap[i * 128:(i + 1) * 128, :])
            bld.waits("vector", [t_gd, t_x])
            bld.op0("vector", lambda e, p4=p4, par=par, i=i:
                    e.scalar_tensor_tensor(
                y0_b[par][:, :], r12_b[p4][:, 0:D], w1h_s[:, i:i + 1],
                xt2_b[par][:, :], ALU.mult, ALU.add))
            t_y = bld.op("vector", lambda e, p4=p4, par=par, i=i:
                         e.scalar_tensor_tensor(
                xt2_b[par][:, :], r12_b[p4][:, D:2 * D], w2h_s[:, i:i + 1],
                y0_b[par][:, :], ALU.mult, ALU.add))
            tk[(ph, "rfree", i)] = t_y
            bld.wait("sync", t_y)
            t_yst = bld.dma("sync", f"qY{par}", y_ap[i * 128:(i + 1) * 128, :],
                            xt2_b[par][:, :])
            tk[(ph, "ycons", i)] = [t_y, t_yst]
        if not keep_scope:
            pes.close()
        tk[(ph, "pes")] = pes
        return [tk.get((ph, "ycons", NT - 1 - k)) for k in range(4)]

    def emit_attn():
        yc_end = [tk.get((1, "ycons", NT - 1 - k)) for k in range(4)]
        expfree = tk.get((1, "expfree"))
        aes = ExitStack()
        winT_s = SB(aes, "winT_s", [128, DT * D], BF16)
        walT_s = SB(aes, "walT_s", [128, DT * D], BF16)
        wbeT_s = SB(aes, "wbeT_s", [128, DT * D], BF16)
        wvT_s = SB(aes, "wvT_s", [128, DT * D], BF16)
        woT_s = walT_s
        NLT = 2
        ln2T = [SB(aes, f"ln2T_{i}", [128, DT * TCH], BF16) for i in range(NLT)]
        xpT = [SB(aes, f"xpT_{i}", [128, DT * TCH], BF16) for i in range(2)]
        aln_b = [SB(aes, f"aln{i}", [128, D], BF16) for i in range(4)]

        bld.waits("gpsimd", [expfree])
        for (src, dst) in [(winT, winT_s), (walT, walT_s), (wbeT, wbeT_s),
                           (wvT, wvT_s)]:
            for a in range(DT):
                tk["attw"] = bld.dma("gpsimd", "qAW", dst[:, a * D:(a + 1) * D],
                                     src.ap()[a * 128:(a + 1) * 128, :])

        tk[(1, "pes")].close()
        NAB = 3
        al_b = [SB(aes, f"al_b{i}", [128, TCH]) for i in range(NAB)]
        be_b = [SB(aes, f"be_b{i}", [128, TCH]) for i in range(NAB)]
        vv_b = [SB(aes, f"vv_b{i}", [128, TCH]) for i in range(NAB)]
        bvb = [SB(aes, f"bvb{i}", [128, TCH]) for i in range(NAB)]
        hscr = [SB(aes, f"hscr{i}", [128, TCH]) for i in range(2)]
        silb = [SB(aes, f"silb{i}", [128, TCH]) for i in range(2)]
        hcarry = SB(aes, "hcarry", [128, FT])
        hcarry2 = hcarry
        g_b = SB(aes, "g_b", [128, FT * TCH], BF16)
        NRL = 3
        al2 = [SB(aes, f"al2_{i}", [128, TCH]) for i in range(NRL)]
        bv2 = [SB(aes, f"bv2_{i}", [128, TCH]) for i in range(NRL)]
        xo_b = [SB(aes, f"xo_{i}", [128, D]) for i in range(2)]
        xr_b = [SB(aes, f"xr_{i}", [128, D]) for i in range(2)]
        hend_s = SB(aes, "hend_s", [128, FT])
        bld.waits("vector", [yc_end])
        t_hc0 = bld.op("vector", lambda e: e.memset(hcarry[:, :], 0.0))
        scan_t = t_hc0
        spill_t = []
        TPC = TCH // 128

        for j in range(TPC):
            r = emit_ln_stats(j, xr1.ap(), aln_b, 4,
                              dep=tk.get((1, "ycons", j)), alt=True)
            tk[("a1lnst", j)] = r["ln"]
        for ch in range(NCH):
            par = ch % NLT
            xpar = ch % 2
            bld.waits("scalar", [tk.get(("a1xpev", ch - NLT))])
            if ch == 0:
                bld.waits("tensor", [tk["attw"]])
            for j in range(TPC):
                i = ch * TPC + j
                t_l2 = emit_ln_tr(
                    i, tk[("a1lnst", i)], aln_b, 4,
                    ln2T[par][:, :].rearrange("p (a t) -> p a t", a=DT)
                    [:, :, j * 128:(j + 1) * 128])
                tk[("ln_free", i)] = [t_l2]
            if ch + 1 < NCH:
                for j in range(TPC):
                    i = (ch + 1) * TPC + j
                    r = emit_ln_stats(i, xr1.ap(), aln_b, 4,
                                      dep=tk.get((1, "ycons", i)), alt=True)
                    tk[("a1lnst", i)] = r["ln"]
            for b in range(DT):
                pp = b % 2
                bld.waits("tensor", [t_l2, tk.get(("a1xpf", pp))])
                for a in range(DT):
                    t_xp = bld.op("tensor", lambda e, a=a, b=b, par=par, pp=pp:
                                  e.matmul(
                        pool[pp][:, 0:TCH],
                        winT_s[:, a * D + b * 128:a * D + (b + 1) * 128],
                        ln2T[par][:, a * TCH:(a + 1) * TCH],
                        start=(a == 0), stop=(a == DT - 1)))
                bld.wait("scalar", t_xp)
                t_xe = bld.op("scalar", lambda e, b=b, xpar=xpar, pp=pp:
                              e.activation(xpT[xpar][:, b * TCH:(b + 1) * TCH],
                                           pool[pp][:, 0:TCH], AF.Silu))
                tk[("a1xpf", pp)] = t_xe
            tk[("a1xpev", ch)] = t_xe
            if ch == 0:
                bld.waits("scalar", [yc_end])
                bld.waits("gpsimd", [yc_end])
            for b in range(FT):
                pp = (ch * FT + b) % NAB
                trips = []
                for (wsb, bias, dstl, fn) in ((walT_s, bal_s, al_b, AF.Sigmoid),
                                              (wbeT_s, bbe_s, be_b, AF.Sigmoid),
                                              (wvT_s, bvv_s, vv_b, AF.Tanh)):
                    agc = tk.get("a1agc", 0)
                    tk["a1agc"] = agc + 1
                    bk = 2 + (agc % 4)
                    bld.waits("tensor", [t_xe, tk.get(("a1bank", bk))])
                    for a in range(DT):
                        t_gm = bld.op("tensor", lambda e, a=a, b=b, xpar=xpar,
                                      bk=bk, wsb=wsb: e.matmul(
                            pool[bk][:, 0:TCH],
                            wsb[:, a * D + b * 128:a * D + (b + 1) * 128],
                            xpT[xpar][:, a * TCH:(a + 1) * TCH],
                            start=(a == 0), stop=(a == DT - 1)))
                    bld.waits("scalar", [t_gm, tk.get(("a1ev", id(dstl), pp))])
                    t_ge = bld.op("scalar", lambda e, b=b, dstl=dstl, fn=fn,
                                  pp=pp, bk=bk, bias=bias: e.activation(
                        dstl[pp][:, :], pool[bk][:, 0:TCH], fn,
                        bias=bias[:, b:b + 1]))
                    tk[("a1bank", bk)] = t_ge
                    tk[("a1gf", id(dstl), pp)] = t_ge
                    if wsb is walT_s:
                        tk[("a1wal_last")] = t_ge
                    trips.append(t_ge)
                bld.waits("vector", [trips[1], trips[2],
                                     tk.get(("a1bvf", pp))])
                t_bv = bld.op("vector", lambda e, pp=pp: e.tensor_tensor(
                    bvb[pp][:, :], be_b[pp][:, :], vv_b[pp][:, :], ALU.mult))
                tk[("a1ev", id(be_b), pp)] = t_bv
                tk[("a1ev", id(vv_b), pp)] = t_bv
                tk[("a1bvt", b)] = [t_bv, trips[0], pp]

                def p1_scan(bb):
                    t_bvb, t_alb, ppb = tk[("a1bvt", bb)]
                    hrb = bb % 2
                    bld.waits("vector", [t_alb, tk.get(("p1c", bb))])
                    t_s = bld.op("vector", lambda e, ppb=ppb, bb=bb, hrb=hrb:
                                 e.tensor_tensor_scan(
                        hscr[hrb][:, :], al_b[ppb][:, :], bvb[ppb][:, :],
                        hcarry[:, bb:bb + 1], ALU.mult, ALU.add))
                    t_c = bld.op("vector", lambda e, bb=bb, hrb=hrb:
                                 e.tensor_copy(hcarry[:, bb:bb + 1],
                                               hscr[hrb][:, TCH - 1:TCH]))
                    tk[("p1c", bb)] = t_c
                    bld.waits("sync", [t_alb, t_c])
                    chb = tk[("a1ch", bb)]
                    t_sp1 = bld.dma("sync", f"qSP{ppb}",
                                    alpha_sp.ap()[bb * 128:(bb + 1) * 128,
                                                  chb * TCH:(chb + 1) * TCH],
                                    al_b[ppb][:, :])
                    t_sp2 = bld.dma("sync", f"qSP{ppb}",
                                    bvv_sp.ap()[bb * 128:(bb + 1) * 128,
                                                chb * TCH:(chb + 1) * TCH],
                                    bvb[ppb][:, :])
                    tk[("a1ev", id(al_b), ppb)] = [t_sp2, t_c]
                    tk[("a1bvf", ppb)] = [t_sp2, t_c]
                    tk[("spill_last", ppb)] = t_sp2

                tk[("a1ch", b)] = ch
                if b > 0:
                    p1_scan(b - 1)
                if b == FT - 1:
                    p1_scan(b)
        spill_t = [tk.get(("spill_last", i)) for i in range(NAB)]

        bld.waits("gpsimd", [tk[("a1wal_last")]])
        for a in range(DT):
            tk["wout"] = bld.dma("gpsimd", "qAW", woT_s[:, a * D:(a + 1) * D],
                                 woT.ap()[a * 128:(a + 1) * 128, :])
        bld.waits("sync", [[tk.get(("p1c", b)) for b in range(FT)]] + spill_t)
        t_he = bld.dma("sync", "qHE", hend_loc.ap(), hcarry[:, :])
        bld.wait("gpsimd", t_he)
        t_agh = bld.collective("AllGather", ALU.bypass,
                               [hend_loc.ap().opt()], [hend_all.ap().opt()],
                               "ccH", GP)
        bld.wait("gpsimd", t_agh)
        t_hl = bld.dma("gpsimd", "qHL", hend_s[:, :], hend_all.ap()[0:128, :])
        bld.waits("vector", [t_hl, [tk.get(("p1c", b)) for b in range(FT)]])
        t_hy = bld.op("vector", lambda e: e.tensor_tensor(
            hcarry[:, :], hend_s[:, :],
            isodd_s[:, 0:1].broadcast_to([128, FT]), ALU.mult))
        for b in range(FT):
            tk[("p2c", b)] = t_hy

        bld.waits("scalar", [spill_t])
        t_st = None
        for ch in range(NCH):
            gdone_t = None
            for b in range(FT):
                pp = b % NRL
                lin_b = ch * FT + b
                bld.waits("sync", [tk.get(("a2rel", lin_b - NRL))])
                ra = bld.dma("sync", f"qRL{pp}", al2[pp][:, :],
                             alpha_sp.ap()[b * 128:(b + 1) * 128,
                                           ch * TCH:(ch + 1) * TCH])
                rb = bld.dma("sync", f"qRL{pp}", bv2[pp][:, :],
                             bvv_sp.ap()[b * 128:(b + 1) * 128,
                                         ch * TCH:(ch + 1) * TCH])
                hr = b % 2
                sr = b % 2
                bld.waits("vector", [ra, rb, tk.get(("p2c", b))])
                t_s = bld.op("vector", lambda e, pp=pp, b=b, hr=hr:
                             e.tensor_tensor_scan(
                    hscr[hr][:, :], al2[pp][:, :], bv2[pp][:, :],
                    hcarry2[:, b:b + 1], ALU.mult, ALU.add))
                scan2_t = bld.op("vector", lambda e, b=b, hr=hr: e.tensor_copy(
                    hcarry2[:, b:b + 1], hscr[hr][:, TCH - 1:TCH]))
                tk[("p2c", b)] = scan2_t
                tk[("a2rel", lin_b)] = scan2_t
                bld.waits("scalar", [t_s, tk.get(("a2silR", sr))])
                t_sil = bld.op("scalar", lambda e, hr=hr, sr=sr: e.activation(
                    silb[sr][:, :], hscr[hr][:, :], AF.Silu))
                tk[("a2silt", b)] = (t_sil, hr, sr)

                def p2_g(bb):
                    t_silb, hrb, srb = tk[("a2silt", bb)]
                    bld.waits("vector", [t_silb, tk.get(("a2oplast", ch - 1))])
                    g_t = bld.op("vector", lambda e, bb=bb, hrb=hrb, srb=srb:
                                 e.tensor_tensor(
                        g_b[:, bb * TCH:(bb + 1) * TCH], hscr[hrb][:, :],
                        silb[srb][:, :], ALU.mult))
                    tk[("a2silR", srb)] = g_t
                    return g_t

                if b > 0:
                    gdone_t = p2_g(b - 1)
                if b == FT - 1:
                    gdone_t = p2_g(b)
            tk[("a2gb", ch)] = gdone_t
            for j in range(TCH // 128):
                i = ch * (TCH // 128) + j
                par = i % 2
                bld.waits("scalar", [tk.get(("a2yc", i - 2))])
                t_xr = bld.dma("scalar", f"qXR{par}", xr_b[par][:, :],
                               xr1.ap()[i * 128:(i + 1) * 128, :])
                bld.waits("tensor", [gdone_t, tk.get(("a2of", i - 2)),
                                     tk["wout"]])
                for dv in range(NDV):
                    for a in range(FT):
                        t_op = bld.op("tensor", lambda e, a=a, j=j, dv=dv,
                                      par=par: e.matmul(
                            pool[par * NDV + dv][:, 0:DHS],
                            g_b[:, a * TCH + j * 128:a * TCH + (j + 1) * 128],
                            woT_s[:, a * D + dv * DHS:a * D + (dv + 1) * DHS],
                            start=(a == 0), stop=(a == FT - 1)))
                bld.waits("vector", [t_op, t_xr, tk.get(("a2yc", i - 2))])
                for dv in range(NDV):
                    t_yv = bld.op("vector", lambda e, par=par, dv=dv:
                                  e.tensor_tensor(
                        xo_b[par][:, dv * DHS:(dv + 1) * DHS],
                        pool[par * NDV + dv][:, 0:DHS],
                        xr_b[par][:, dv * DHS:(dv + 1) * DHS], ALU.add))
                tk[("a2of", i)] = t_yv
                tk[("a2oplast", ch)] = t_op
                bld.wait("sync", t_yv)
                t_st = bld.dma("sync", f"qY{par}", xr2.ap()[i * 128:(i + 1) * 128, :],
                               xo_b[par][:, :])
                tk[("a2yc", i)] = [t_yv, t_st]
        aes.close()
        tk["att_done"] = [tk[("a2yc", N // 128 - 1)],
                          tk.get(("a2yc", N // 128 - 2)),
                          tk.get(("a2of", N // 128 - 1)),
                          tk.get(("a2silR", 0)), tk.get(("a2silR", 1))]
        return [tk[("a2yc", N // 128 - 1)], tk.get(("a2yc", N // 128 - 2))]

    emit_setup()
    if phases == "1A2":
        t_f1 = emit_ffn(1, x_in.ap(), xr1.ap())
        bld.waits("sync", [t_f1])
        bld.barrier()
        t_at = emit_attn()
        bld.waits("sync", [t_at])
        bld.barrier()
        t_f2 = emit_ffn(2, xr2.ap(), y_out.ap())
        bld.waits("gpsimd", [t_f2])
        bld.waits("sync", [t_f2])
        bld.barrier()
    elif phases == "1":
        t_f1 = emit_ffn(1, x_in.ap(), y_out.ap())
        bld.waits("gpsimd", [t_f1])
        bld.waits("sync", [t_f1])
        bld.barrier()
    elif phases == "1A":
        t_f1 = emit_ffn(1, x_in.ap(), xr1.ap())
        bld.waits("sync", [t_f1])
        bld.barrier()
        t_at = emit_attn()
        bld.waits("gpsimd", [t_at])
        bld.waits("sync", [t_at])
        bld.barrier()

    with nc.Block() as blk:
        for eng in ENGS:
            fns = bld.q[eng]
            if fns:
                def body(e, fns=fns):
                    for f in fns:
                        f(e)
                getattr(blk, eng)(body)
    root.close()
    return nc, c



def make_inputs_per_core(c, full):
    D, E, CAP, N, NT, FT = c["D"], c["E"], c["CAP"], c["N"], c["NT"], c["DT"]
    HC2 = (3 * CAP // 4) // 128 * 128
    x = np.ascontiguousarray(
        np.asarray(full["L0_attn_x"]).astype(np.float32, copy=False)
        .reshape(c["TT"], D))
    f32 = lambda k: np.asarray(full[k]).astype(np.float32, copy=False)
    w_in, w_al, w_be, w_v, w_o = (f32("L0_attn_W_in"), f32("L0_attn_W_alpha"),
                                  f32("L0_attn_W_beta"), f32("L0_attn_W_v"),
                                  f32("L0_attn_W_out"))
    b_al, b_be, b_v = (f32("L0_attn_b_alpha"), f32("L0_attn_b_beta"),
                       f32("L0_attn_b_v"))
    wg, bg = f32("L0_ffn_Wg"), f32("L0_ffn_bg")
    w1, w2, w3 = f32("L0_ffn_W1"), f32("L0_ffn_W2"), f32("L0_ffn_W3")

    bf = ml_dtypes.bfloat16
    lay_bias = lambda b: np.ascontiguousarray(b.reshape(FT, 128).T)
    common = {
        "winT": np.ascontiguousarray(w_in.T),
        "walT": np.ascontiguousarray(w_al.T),
        "wbeT": np.ascontiguousarray(w_be.T),
        "wvT": np.ascontiguousarray(w_v.T),
        "woT": np.ascontiguousarray(w_o.T),
        "b_al": lay_bias(b_al), "b_be": lay_bias(b_be), "b_vv": lay_bias(b_v),
        "wg": np.ascontiguousarray(wg), "bg": np.ascontiguousarray(bg.reshape(1, E)),
        "idb": np.eye(128, dtype=bf),
        "idf": np.eye(128, dtype=np.float32),
        "u128": np.triu(np.ones((128, 128), np.float32)).astype(bf),
        "ones1": np.ones((1, 128), np.float32),
        "onescol": np.ones((128, 1), np.float32).astype(bf),
        "ecap8": np.ascontiguousarray(np.broadcast_to(
            (np.arange(E) * HC2).astype(np.float32), (128, E))),
        "ecapb": np.ascontiguousarray(np.broadcast_to(
            (np.arange(E) * CAP).astype(np.float32), (128, E))),
        "eoff8": np.ascontiguousarray(np.broadcast_to(
            ((E - 1) * HC2 + np.arange(E) * (CAP + 1 - 2 * HC2))
            .astype(np.float32), (128, E))),
    }
    maps = []
    for cid in range(NCORES):
        sel = np.zeros((E, 1), np.float32)
        sel[:cid] = 1.0
        l = np.arange(N, dtype=np.int64)
        NQ = N // 4
        row = (l // NQ) * (NCORES * NQ) + cid * NQ + (l % NQ)
        tok = np.ascontiguousarray(
            (row.astype(np.float32) + 1.0).reshape(NT, 128).T)
        m = dict(common)
        m.update({
            "x": np.ascontiguousarray(x[cid * N:(cid + 1) * N]),
            "w1": np.ascontiguousarray(w1[cid]),
            "w2": np.ascontiguousarray(w2[cid]),
            "w3": np.ascontiguousarray(w3[cid]),
            "selc": sel,
            "isodd": np.full((128, 1), float(cid % 2), np.float32),
            "tokids": tok,
        })
        maps.append(m)
    return maps


_CACHE = {}


def kernel(**inputs):
    c = derive(FULL)
    if "nc" not in _CACHE:
        _CACHE["nc"] = build_program(FULL)[0]
    nc = _CACHE["nc"]
    maps = make_inputs_per_core(c, inputs)
    res = run_bass_kernel_spmd(nc, maps, core_ids=list(range(NCORES)))
    out = np.concatenate([np.asarray(res.results[i]["y"]).astype(np.float32)
                          for i in range(NCORES)], axis=0)
    return out.reshape(c["B"], c["TSEQ"], c["D"])


# version 23 (final)
# speedup vs baseline: 1.0434x; 1.0434x over previous
import sys
if "/opt/trn_rl_repo" not in sys.path:
    sys.path.insert(0, "/opt/trn_rl_repo")

from contextlib import ExitStack

import numpy as np
import ml_dtypes
import concourse.bass as bass
import concourse.mybir as mybir
from concourse.bass_utils import run_bass_kernel_spmd
from bass_rust import IndirectOffsetOnAxis

F32 = mybir.dt.float32
BF16 = mybir.dt.bfloat16
U32 = mybir.dt.uint32
AF = mybir.ActivationFunctionType
ALU = mybir.AluOpType
AXX = mybir.AxisListType.X

NCORES = 8
BIG = 65536.0
EPS = 1e-5

FULL = dict(D=1024, H=4096, E=8, B=4, TSEQ=8192, CAPF=1.25)


def derive(cfg):
    c = dict(cfg)
    c["TT"] = c["B"] * c["TSEQ"]
    c["N"] = c["TT"] // NCORES
    c["CAP"] = int(c["TT"] / c["E"] * c["CAPF"])
    assert c["CAP"] % 128 == 0 and (c["E"] * c["CAP"] // 128) % NCORES == 0
    c["DT"] = c["D"] // 128
    c["HT"] = c["H"] // 128
    c["NT"] = c["N"] // 128
    c["TCH"] = min(512, c["N"])
    c["NCH"] = c["N"] // c["TCH"]
    c["SCH"] = min(640, c["CAP"])
    assert c["CAP"] % c["SCH"] == 0 and c["SCH"] % 128 == 0
    c["NSC"] = c["CAP"] // c["SCH"]
    c["HB"] = min(512, c["H"])
    assert c["H"] % c["HB"] == 0
    c["GPC"] = c["CAP"] // 128
    c["DHS"] = min(512, c["D"])
    c["NDV"] = c["D"] // c["DHS"]
    return c


def split_sv(sch):
    out, v = [], 0
    while v < sch:
        r = sch - v
        s = 512 if r >= 512 else r
        if r == 640:
            s = 384
        out.append((v, s))
        v += s
    return out


ENGS = ("sync", "tensor", "vector", "scalar", "gpsimd")


class Bld:
    def __init__(self, nc, es):
        self.nc = nc
        self.q = {e: [] for e in ENGS}
        self.sems = {}
        self.cnt = {}
        self.es = es

    def sem(self, name):
        if name not in self.sems:
            self.sems[name] = self.es.enter_context(self.nc.semaphore(name))
            self.cnt[name] = 0
        return self.sems[name]

    def dma(self, eng, ch, out, in_, **kw):
        h = self.sem(ch)
        self.cnt[ch] += 16
        t = self.cnt[ch]

        def f(e, out=out, in_=in_, h=h, kw=kw):
            e.dma_start(out=out, in_=in_, **kw).then_inc(h, 16)
        self.q[eng].append(f)
        return (ch, t)

    def idma(self, ch, out, out_offset, in_, in_offset, bounds):
        h = self.sem(ch)
        self.cnt[ch] += 16
        t = self.cnt[ch]
        if not hasattr(self, "bregs"):
            self.bregs = {}

        def f(e, out=out, oo=out_offset, in_=in_, io=in_offset, b=bounds, h=h):
            if b not in self.bregs:
                r = e.alloc_register()
                e.reg_mov(r, b)
                self.bregs[b] = r
            e.indirect_dma_start(out=out, out_offset=oo, in_=in_, in_offset=io,
                                 bounds_check=self.bregs[b],
                                 oob_is_err=False).then_inc(h, 16)
        self.q["gpsimd"].append(f)
        return (ch, t)

    def op(self, eng, fn):
        ch = {"tensor": "cP", "vector": "cV", "scalar": "cA", "gpsimd": "cG"}[eng]
        h = self.sem(ch)
        self.cnt[ch] += 1
        t = self.cnt[ch]
        chain = eng != "tensor"

        def f(e, fn=fn, h=h, t=t, chain=chain):
            if chain and t > 1:
                e.wait_ge(h, t - 1)
            fn(e).then_inc(h, 1)
        self.q[eng].append(f)
        return (ch, t)

    def op0(self, eng, fn):
        self.op(eng, fn)

    def wait(self, eng, ticket):
        if not ticket:
            return
        ch, t = ticket
        h = self.sem(ch)

        def f(e, h=h, t=t):
            e.wait_ge(h, t)
        self.q[eng].append(f)

    def waits(self, eng, tickets):
        best = {}

        def add(tkt):
            if not tkt:
                return
            if isinstance(tkt, list):
                for t2 in tkt:
                    add(t2)
                return
            ch, t = tkt
            if ch not in best or best[ch] < t:
                best[ch] = t
        add(list(tickets))
        for ch, t in best.items():
            self.wait(eng, (ch, t))

    def collective(self, kind, op, ins, outs, name, groups):
        h = self.sem(name)
        self.cnt[name] += 1
        t = self.cnt[name]

        def f(e, kind=kind, op=op, ins=ins, outs=outs, h=h, groups=groups):
            e.collective_compute(kind, op, replica_groups=groups,
                                 ins=ins, outs=outs).then_inc(h, 1)
        self.q["gpsimd"].append(f)
        return (name, t)

    def barrier(self):
        h = self.sem("bar")
        self.cnt["bar"] += len(ENGS)
        t = self.cnt["bar"]
        for eng in ENGS:
            def f(e, h=h, t=t):
                e.sem_inc(h, 1)
                e.wait_ge(h, t)
            self.q[eng].append(f)


def build_program(cfg, phases="1A2"):
    c = derive(cfg)
    D, H, E, CAP = c["D"], c["H"], c["E"], c["CAP"]
    DT, HT, NT, N, TT = c["DT"], c["HT"], c["NT"], c["N"], c["TT"]
    TCH, NCH, SCH, NSC, HB, GPC = (c["TCH"], c["NCH"], c["SCH"], c["NSC"],
                                   c["HB"], c["GPC"])
    DHS, NDV = c["DHS"], c["NDV"]
    FT = DT
    SV = split_sv(SCH)
    G8 = [list(range(NCORES))]
    GP = [[i, i + 1] for i in range(0, NCORES, 2)]
    GPT = SCH // 128
    JT = HB // 128
    NHB = H // HB
    HCAP = (3 * CAP // 4) // 128 * 128
    S_HALF = (HCAP + SCH - 1) // SCH

    nc = bass.Bass("TRN2", target_bir_lowering=False)
    root = ExitStack()
    bld = Bld(nc, root)
    tk = {}

    inp = lambda name, shape, dt=F32: nc.dram_tensor(name, shape, dt,
                                                     kind="ExternalInput")
    x_in = inp("x", [N, D])
    winT = inp("winT", [D, D]); walT = inp("walT", [D, D])
    wbeT = inp("wbeT", [D, D]); wvT = inp("wvT", [D, D]); woT = inp("woT", [D, D])
    b_al = inp("b_al", [128, FT]); b_be = inp("b_be", [128, FT])
    b_vv = inp("b_vv", [128, FT])
    wg = inp("wg", [D, E]); bg = inp("bg", [1, E])
    w1 = inp("w1", [D, H]); w2 = inp("w2", [D, H]); w3 = inp("w3", [H, D])
    idb = inp("idb", [128, 128], BF16)
    idf = inp("idf", [128, 128])
    u128 = inp("u128", [128, 128], BF16)
    ones1 = inp("ones1", [1, 128])
    onescol = inp("onescol", [128, 1], BF16)
    selc = inp("selc", [E, 1])
    isodd = inp("isodd", [128, 1])
    tokids = inp("tokids", [128, NT])
    ecap8 = inp("ecap8", [128, E])
    ecapb = inp("ecapb", [128, E])
    eoff8 = inp("eoff8", [128, E])
    y_out = nc.dram_tensor("y", [N, D], F32, kind="ExternalOutput")

    scr = lambda name, shape, dt=F32: nc.dram_tensor(name, shape, dt)
    shr = lambda name, shape, dt=F32: nc.dram_tensor(name, shape, dt,
                                                     addr_space="Shared")
    w1b = scr("w1b", [D, H], BF16)
    w2b = scr("w2b", [D, H], BF16)
    w3b = scr("w3b", [H, D], BF16)
    lnx_loc = scr("lnx_loc", [N, D], BF16)
    ag_lnx = shr("ag_lnx", [TT, D], BF16)
    cnt_loc = scr("cnt_loc", [1, E])
    cnt_all = shr("cnt_all", [NCORES, E])
    slot_loc = scr("slot_loc", [E * CAP // 128, 128])
    slot_sh = scr("slot_sh", [GPC, 128])
    out_loc = scr("out_loc", [CAP + 1, D], BF16)
    out_all = shr("out_all", [E * (CAP + 1), D], BF16)
    xr1 = scr("xr1", [N, D])
    xr2 = scr("xr2", [N, D])
    alpha_sp = scr("alpha_sp", [FT * 128, N])
    bvv_sp = scr("bvv_sp", [FT * 128, N])
    hend_loc = scr("hend_loc", [128, FT])
    hend_all = scr("hend_all", [2 * 128, FT])

    def SB(stack, name, shape, dt=F32):
        return stack.enter_context(nc.sbuf_tensor(name, shape, dt))

    idb_s = SB(root, "idb_s", [128, 128], BF16)
    idf_s = SB(root, "idf_s", [128, 128])
    u128_s = SB(root, "u128_s", [128, 128], BF16)
    ones1_s = SB(root, "ones1_s", [1, 128])
    onescol_s = SB(root, "onescol_s", [128, 1], BF16)
    selc_s = SB(root, "selc_s", [E, 1])
    isodd_s = SB(root, "isodd_s", [128, 1])
    tokids_s = SB(root, "tokids_s", [128, NT])
    ecap8_s = SB(root, "ecap8_s", [128, E])
    ecapb_s = SB(root, "ecapb_s", [128, E])
    eoff8_s = SB(root, "eoff8_s", [128, E])
    bal_s = SB(root, "bal_s", [128, FT])
    bbe_s = SB(root, "bbe_s", [128, FT])
    bvv_s = SB(root, "bvv_s", [128, FT])
    wg_s = SB(root, "wg_s", [128, DT * E], BF16)
    bg_s = SB(root, "bg_s", [1, E])
    bgt_s = SB(root, "bgt_s", [128, E])
    zeros_s = SB(root, "zeros_s", [128, E * CAP // 128])
    NXB = 3
    xt_b = [SB(root, f"xt{i}", [128, D]) for i in range(NXB)]
    sq_b = [SB(root, f"sq{i}", [128, D]) for i in range(1)]
    red_b = [SB(root, f"red{i}", [128, 1]) for i in range(2)]
    ss_b = [SB(root, f"ss{i}", [128, 1]) for i in range(2)]
    sd_b = [SB(root, f"sd{i}", [128, 1]) for i in range(2)]
    mneg_b = [SB(root, f"mneg{i}", [128, 1]) for i in range(2)]
    rstd_b = [SB(root, f"rstd{i}", [128, 1]) for i in range(2)]
    ln_b = [SB(root, f"ln{i}", [128, D], BF16) for i in range(2)]
    lnT_b = [SB(root, f"lnT{i}", [128, DT * 128], BF16) for i in range(2)]
    STGW = min(2048, max(H, D))
    stg_b = [SB(root, f"stg{i}", [128, STGW], BF16) for i in range(2)]
    eps_s = SB(root, "eps_s", [128, 1])

    psGC = root.enter_context(nc.psum_tensor("psGC", [128, 512], F32))
    psT = root.enter_context(nc.psum_tensor("psT", [128, DT * 128], BF16))
    pool = [root.enter_context(nc.psum_tensor(f"pool{i}", [128, 512], F32))
            for i in range(6)]

    CUM0 = 256

    def emit_setup():
        pairs = [(idb_s, idb), (idf_s, idf), (u128_s, u128), (ones1_s, ones1),
                 (onescol_s, onescol), (selc_s, selc), (isodd_s, isodd),
                 (tokids_s, tokids), (ecap8_s, ecap8), (ecapb_s, ecapb),
                 (eoff8_s, eoff8),
                 (bal_s, b_al),
                 (bbe_s, b_be), (bvv_s, b_vv), (bg_s, bg)]
        for s, d in pairs:
            tk["consts"] = bld.dma("sync", "qCN", s[:, :], d.ap())
        for a in range(DT):
            tk["wg"] = bld.dma("gpsimd", "qWG", wg_s[:, a * E:(a + 1) * E],
                               wg.ap()[a * 128:(a + 1) * 128, :])
        bld.waits("tensor", [tk["consts"], tk["wg"]])
        t_bm = bld.op("tensor", lambda e: e.matmul(
            psGC[:, 0:E], ones1_s[0:1, :], bg_s[0:1, :], start=True, stop=True))
        bld.wait("vector", t_bm)
        tk["bgt"] = bld.op("vector", lambda e: e.tensor_copy(bgt_s[:, :],
                                                             psGC[:, 0:E]))
        bld.op0("vector", lambda e: e.memset(eps_s[:, :], EPS))
        bld.op0("vector", lambda e: e.memset(stg_b[0][0:1, 0:D], 0.0))
        tk["zeros"] = bld.op("vector", lambda e: e.memset(zeros_s[:, :], 0.0))
        bld.wait("vector", tk["zeros"])
        bld.wait("scalar", tk["zeros"])
        bld.wait("gpsimd", tk["zeros"])
        tk["zrow"] = bld.dma("gpsimd", "qZR", out_loc.ap()[CAP:CAP + 1, :],
                             stg_b[0][0:1, 0:D])

    def emit_precast(part):
        cons = tk.get("pc_cons", [tk.get("zrow"), None])
        j = tk.get("pc_j", 0)

        def pair(src, dst, a, c0, cw):
            nonlocal j
            b = j % 2
            bld.waits("gpsimd", [cons[b]])
            tin = bld.dma("gpsimd", f"qSTG{b}", stg_b[b][:, 0:cw],
                          src.ap()[a * 128:(a + 1) * 128, c0:c0 + cw])
            bld.wait("scalar", tin)
            cons[b] = bld.dma(
                "scalar", f"qSTS{b}",
                dst.ap()[a * 128:(a + 1) * 128, c0:c0 + cw],
                stg_b[b][:, 0:cw])
            j += 1
            return cons[b]

        blocks = ([0] if part in ("A",) else
                  list(range(1, (H + STGW - 1) // STGW)) if part == "B" else [])
        for blk_i in blocks:
            c0 = blk_i * STGW
            cw = min(STGW, H - c0)
            for a in range(DT):
                pair(w1, w1b, a, c0, cw)
            for a in range(DT):
                pair(w2, w2b, a, c0, cw)
            tk[("wfc", blk_i)] = [cons[0], cons[1]]
        if part == "C":
            for a in range(HT):
                tk[("w3r", a)] = pair(w3, w3b, a, 0, D)
        tk["wffn"] = [cons[0], cons[1]]
        tk["pc_cons"] = cons
        tk["pc_j"] = j

    def emit_ln_stats(i, x_ap, lnb, rb, dep=None, alt=False):
        xb = xt_b[i % NXB]
        r = i % 2
        lr = i % rb
        veng = "gpsimd" if alt else "vector"
        bld.waits("sync", [tk.get(("xt_free", i - NXB)), dep])
        tx = bld.dma("sync", f"qX{i % NXB}", xb[:, :], x_ap[i * 128:(i + 1) * 128, :])
        bld.waits("scalar", [tx, tk.get(("stats_free", i - 2))])
        t_red = bld.op("scalar", lambda e, xb=xb, r=r: e.activation(
            sq_b[0][:, :], xb[:, :], AF.Identity, accum_out=red_b[r][:, 0:1]))
        bld.waits(veng, [t_red])
        t_mn = bld.op(veng, lambda e, r=r: e.tensor_scalar(
            mneg_b[r][:, :], red_b[r][:, :], -1.0 / D, None, ALU.mult))
        bld.waits("scalar", [t_mn])
        t_sq = bld.op("scalar", lambda e, xb=xb, r=r: e.activation(
            sq_b[0][:, :], xb[:, :], AF.Square, bias=mneg_b[r][:, 0:1],
            scale=1.0, accum_out=ss_b[r][:, 0:1]))
        bld.wait("scalar", t_sq)
        if alt:
            t_sd = bld.op("scalar", lambda e, r=r: e.activation(
                sd_b[r][:, :], ss_b[r][:, :], AF.Ln, bias=eps_s[:, 0:1],
                scale=1.0 / D))
            bld.wait("scalar", t_sd)
            t_rs = bld.op("scalar", lambda e, r=r: e.activation(
                rstd_b[r][:, :], sd_b[r][:, :], AF.Exp, scale=-0.5))
        else:
            t_sd = bld.op("scalar", lambda e, r=r: e.activation(
                sd_b[r][:, :], ss_b[r][:, :], AF.Sqrt, bias=eps_s[:, 0:1],
                scale=1.0 / D))
            bld.wait("vector", t_sd)
            t_rs = bld.op("vector", lambda e, r=r: e.reciprocal(
                rstd_b[r][:, :], sd_b[r][:, :]))
        bld.waits(veng, [t_rs, tk.get(("ln_free", i - rb))])
        t_ln = bld.op(veng, lambda e, xb=xb, r=r, lnb=lnb, lr=lr:
                      e.tensor_scalar(
            lnb[lr][:, :], xb[:, :], mneg_b[r][:, 0:1], rstd_b[r][:, 0:1],
            ALU.add, ALU.mult))
        tk[("xt_free", i)] = t_ln
        tk[("stats_free", i)] = t_ln
        return {"ln": t_ln, "lr": lr}

    def emit_ln_tr(i, t_ln, lnb, rb, lnT_dst):
        lr = i % rb
        bld.waits("tensor", [t_ln, tk.get("psT_free")])
        for a in range(DT):
            t_tp = bld.op("tensor", lambda e, a=a, lnb=lnb, lr=lr: e.transpose(
                psT[:, a * 128:(a + 1) * 128],
                lnb[lr][:, a * 128:(a + 1) * 128], idb_s[:, :]))
        bld.wait("scalar", t_tp)
        t_ev = bld.op("scalar", lambda e, d=lnT_dst: e.copy(d, psT[:, :]))
        tk["psT_free"] = t_ev
        return t_ev

    def emit_ln_tile(i, x_ap, store_dram, lnT_dst, alt=False, dep=None):
        out = emit_ln_stats(i, x_ap, ln_b, 2, alt=alt, dep=dep)
        t_ln = out["ln"]
        out["free"] = [t_ln]
        if store_dram is not None:
            bld.wait("sync", t_ln)
            out["store"] = bld.dma("sync", f"qLS{i % 2}",
                                   store_dram[i * 128:(i + 1) * 128, :],
                                   ln_b[i % 2][:, :])
            out["free"].append(out["store"])
        if lnT_dst is not None:
            t_ev = emit_ln_tr(i, t_ln, ln_b, 2, lnT_dst)
            out["lnT"] = t_ev
            out["free"].append(t_ev)
        tk[("ln_free", i)] = out["free"]
        return out

    def emit_ffn(ph, x_ap, y_ap, keep_scope=False, dep_fn=None):
        pes = ExitStack()
        logits_s = SB(pes, f"logits_s{ph}", [128, NT * E])
        m1_s = SB(pes, f"m1_s{ph}", [128, NT])
        m2_s = SB(pes, f"m2_s{ph}", [128, NT])
        w1r_s = SB(pes, f"w1r_s{ph}", [128, NT])
        mask1_s = SB(pes, f"mask1_s{ph}", [128, NT * E], BF16)
        mask2_s = SB(pes, f"mask2_s{ph}", [128, NT * E], BF16)
        label_s = SB(pes, f"label_s{ph}", [128, NT * E], BF16)
        masked_s = SB(pes, f"masked_s{ph}", [128, NT * E])
        seq_s = SB(pes, f"seq_s{ph}", [128, NT * E])
        va_s = SB(pes, f"va_s{ph}", [128, NT * E])
        valid_s = SB(pes, f"valid_s{ph}", [128, NT * E])
        fxf_s = SB(pes, f"fxf_s{ph}", [128, NT * E])
        ffa_s = SB(pes, f"ffa_s{ph}", [128, NT * E])
        fidx1_s = SB(pes, f"fidx1_s{ph}", [128, NT])
        fidx2_s = SB(pes, f"fidx2_s{ph}", [128, NT])
        fidx1_u = SB(pes, f"fidx1_u{ph}", [128, NT], U32)
        fidx2_u = SB(pes, f"fidx2_u{ph}", [128, NT], U32)
        k1_s = SB(pes, f"k1_s{ph}", [128, NT])
        k2_s = SB(pes, f"k2_s{ph}", [128, NT])
        k1_u = SB(pes, f"k1_u{ph}", [128, NT], U32)
        k2_u = SB(pes, f"k2_u{ph}", [128, NT], U32)
        w1h_s = SB(pes, f"w1h_s{ph}", [128, NT])
        w2h_s = SB(pes, f"w2h_s{ph}", [128, NT])
        run_s = SB(pes, f"run_s{ph}", [1, 2 * E])
        cnts_s = SB(pes, f"cnts_s{ph}", [NCORES, E])
        slotsh_s = SB(pes, f"slotsh_s{ph}", [GPC, 128])
        sidxf_s = SB(pes, f"sidxf_s{ph}", [128, GPC])
        sidx_u = SB(pes, f"sidx_u{ph}", [128, GPC], U32)
        NQ = N // 4
        ln_stores = []
        t_agl = []
        for i in range(NT):
            bld.waits("scalar", [tk.get((ph, "gate", i - 2))])
            r = emit_ln_tile(i, x_ap, lnx_loc.ap(), lnT_b[i % 2][:, :],
                             alt=(ph == 2),
                             dep=dep_fn(i) if dep_fn else None)
            ln_stores.append(r["store"])
            if i == NT // 4 - 1 or i == NT // 2 - 1:
                q = (i + 1) // (NT // 4) - 1
                bld.waits("gpsimd", ln_stores)
                t_agl.append(bld.collective(
                    "AllGather", ALU.bypass,
                    [lnx_loc.ap()[q * NQ:(q + 1) * NQ, :].opt()],
                    [ag_lnx.ap()[NCORES * q * NQ:NCORES * (q + 1) * NQ,
                                 :].opt()], f"ccL{ph}{'ab'[q]}", G8))
            bld.waits("tensor", [r["lnT"]])
            if i == 0:
                bld.waits("tensor", [tk["wg"], tk["bgt"], tk["consts"]])
            for a in range(DT):
                t_g = bld.op("tensor", lambda e, a=a, i=i, rr=i % 2: e.matmul(
                    psGC[:, i * E:(i + 1) * E],
                    lnT_b[rr][:, a * 128:(a + 1) * 128],
                    wg_s[:, a * E:(a + 1) * E],
                    start=(a == 0), stop=(a == DT - 1)))
            tk[(ph, "gate", i)] = t_g
        tk[(ph, "lnst")] = ln_stores
        bld.waits("vector", [t_g])
        l3 = logits_s[:, :].rearrange("p (i e) -> p i e", e=E)
        t_logev = bld.op("vector", lambda e: e.tensor_tensor(
            l3, psGC[:, 0:NT * E].rearrange("p (i e) -> p i e", e=E),
            bgt_s[:, :].unsqueeze(1).broadcast_to([128, NT, E]), ALU.add))
        bld.op0("vector", lambda e: e.tensor_reduce(
            m1_s[:, :], l3, axis=AXX, op=ALU.max))
        m1b = m1_s[:, :].unsqueeze(2).broadcast_to([128, NT, E])
        mk1 = mask1_s[:, :].rearrange("p (i e) -> p i e", e=E)
        bld.op0("vector", lambda e: e.tensor_tensor(mk1, l3, m1b, ALU.is_equal))
        md3 = masked_s[:, :].rearrange("p (i e) -> p i e", e=E)
        bld.op0("vector", lambda e: e.scalar_tensor_tensor(
            md3, mk1, -1e30, l3, ALU.mult, ALU.add))
        bld.op0("vector", lambda e: e.tensor_reduce(
            m2_s[:, :], md3, axis=AXX, op=ALU.max))
        m2b = m2_s[:, :].unsqueeze(2).broadcast_to([128, NT, E])
        mk2 = mask2_s[:, :].rearrange("p (i e) -> p i e", e=E)
        bld.op0("vector", lambda e: e.tensor_tensor(mk2, md3, m2b, ALU.is_equal))
        lb3 = label_s[:, :].rearrange("p (i e) -> p i e", e=E)
        t_lab = bld.op("vector", lambda e: e.tensor_tensor(lb3, mk1, mk2, ALU.add))
        d12_s = masked_s
        t_d12 = bld.op("vector", lambda e: e.tensor_tensor(
            d12_s[:, 0:NT], m1_s[:, :], m2_s[:, :], ALU.subtract))
        bld.wait("scalar", t_d12)
        t_w1r = bld.op("scalar", lambda e: e.activation(
            w1r_s[:, :], d12_s[:, 0:NT], AF.Sigmoid))
        bld.wait("vector", t_w1r)
        t_wh = bld.op("vector", lambda e: e.tensor_scalar(
            w1h_s[:, :], w1r_s[:, :], 0.5, None, ALU.mult))
        bld.wait("vector", t_wh)
        t_wh2 = bld.op("vector", lambda e: e.tensor_scalar(
            w2h_s[:, :], w1h_s[:, :], 0.5, -1.0, ALU.subtract, ALU.mult))
        bld.waits("tensor", [t_lab])
        for i in range(NT):
            t_cm = bld.op("tensor", lambda e, i=i: e.matmul(
                psGC[0:1, 0:E], onescol_s[:, :], label_s[:, i * E:(i + 1) * E],
                start=(i == 0), stop=(i == NT - 1)))
        bld.wait("vector", t_cm)
        t_cev = bld.op("vector", lambda e: e.tensor_copy(
            run_s[0:1, 0:E], psGC[0:1, 0:E]))
        bld.wait("sync", t_cev)
        t_cst = bld.dma("sync", "qCT", cnt_loc.ap(), run_s[0:1, 0:E])
        bld.waits("gpsimd", [t_cst])
        t_agc = bld.collective("AllGather", ALU.bypass,
                               [cnt_loc.ap().opt()], [cnt_all.ap().opt()],
                               f"ccC{ph}", G8)
        bld.wait("gpsimd", t_agc)
        bld.waits("gpsimd", [tk.get("att_done")])
        t_cld = bld.dma("gpsimd", "qCT2", cnts_s[:, :], cnt_all.ap())
        NQ2 = N // 4
        bld.waits("gpsimd", [tk[(ph, "lnst")][:3 * NT // 4]])
        t_agl.append(bld.collective(
            "AllGather", ALU.bypass, [lnx_loc.ap()[2 * NQ2:3 * NQ2, :].opt()],
            [ag_lnx.ap()[NCORES * 2 * NQ2:NCORES * 3 * NQ2, :].opt()],
            f"ccL{ph}c", G8))
        bld.waits("tensor", [t_cld])
        t_bse = bld.op("tensor", lambda e: e.matmul(
            psGC[0:1, 8:8 + E], selc_s[:, :], cnts_s[:, :],
            start=True, stop=True))
        bld.wait("vector", t_bse)
        t_run = bld.op("vector", lambda e: e.tensor_copy(
            run_s[0:1, 0:E], psGC[0:1, 8:8 + E]))
        for i in range(NT):
            par, nxt = (i % 2) * E, ((i + 1) % 2) * E
            bld.wait("tensor", t_run)
            bld.op0("tensor", lambda e, i=i: e.matmul(
                psGC[:, CUM0 + i * E:CUM0 + (i + 1) * E], u128_s[:, :],
                label_s[:, i * E:(i + 1) * E], start=True, stop=False))
            bld.op0("tensor", lambda e, i=i, par=par: e.matmul(
                psGC[:, CUM0 + i * E:CUM0 + (i + 1) * E], ones1_s[0:1, :],
                run_s[0:1, par:par + E], start=False, stop=True))
            t_tot = bld.op("tensor", lambda e, i=i: e.matmul(
                psGC[0:1, 16:16 + E], onescol_s[:, :],
                label_s[:, i * E:(i + 1) * E], start=True, stop=True))
            bld.wait("vector", t_tot)
            t_run = bld.op("vector", lambda e, par=par, nxt=nxt: e.tensor_tensor(
                run_s[0:1, nxt:nxt + E], run_s[0:1, par:par + E],
                psGC[0:1, 16:16 + E], ALU.add))
        bld.wait("vector", t_run)
        cum3 = psGC[:, CUM0:CUM0 + NT * E].rearrange("p (i e) -> p i e", e=E)
        sq3 = seq_s[:, :].rearrange("p (i e) -> p i e", e=E)
        bld.op0("vector", lambda e: e.tensor_tensor(sq3, cum3, lb3, ALU.mult))
        bld.op0("vector", lambda e: e.tensor_scalar(
            va_s[:, :], seq_s[:, :], 1.0, None, ALU.is_ge))
        bld.op0("vector", lambda e: e.scalar_tensor_tensor(
            valid_s[:, :], seq_s[:, :], float(CAP), va_s[:, :],
            ALU.is_le, ALU.mult))
        u3 = fxf_s[:, :].rearrange("p (i e) -> p i e", e=E)
        bld.op0("vector", lambda e: e.tensor_scalar(
            ffa_s[:, :], seq_s[:, :], float(HCAP), None, ALU.is_gt))
        bld.op0("vector", lambda e: e.tensor_tensor(
            u3, sq3, ecap8_s[:, :].unsqueeze(1).broadcast_to([128, NT, E]),
            ALU.add))
        TRASH = float(E * HCAP + (CAP - HCAP))
        bld.op0("vector", lambda e: e.tensor_tensor(
            va_s[:, :].rearrange("p (i e) -> p i e", e=E),
            ffa_s[:, :].rearrange("p (i e) -> p i e", e=E),
            eoff8_s[:, :].unsqueeze(1).broadcast_to([128, NT, E]), ALU.mult))
        bld.op0("vector", lambda e: e.tensor_tensor(
            fxf_s[:, :], fxf_s[:, :], va_s[:, :], ALU.add))
        bld.op0("vector", lambda e: e.tensor_scalar(
            fxf_s[:, :], fxf_s[:, :], TRASH + 1.0, None, ALU.subtract))
        bld.op0("vector", lambda e: e.tensor_tensor(
            fxf_s[:, :], valid_s[:, :], fxf_s[:, :], ALU.mult))
        bld.op0("vector", lambda e: e.tensor_scalar(
            fxf_s[:, :], fxf_s[:, :], TRASH, None, ALU.add))
        fx3 = fxf_s[:, :].rearrange("p (i e) -> p i e", e=E)
        ff3 = ffa_s[:, :].rearrange("p (i e) -> p i e", e=E)
        bld.op0("vector", lambda e: e.tensor_tensor(ff3, fx3, mk1, ALU.mult))
        bld.op0("vector", lambda e: e.tensor_reduce(
            fidx1_s[:, :], ff3, axis=AXX, op=ALU.add))
        bld.op0("vector", lambda e: e.tensor_tensor(ff3, fx3, mk2, ALU.mult))
        t_f2r = bld.op("vector", lambda e: e.tensor_reduce(
            fidx2_s[:, :], ff3, axis=AXX, op=ALU.add))
        bld.wait("vector", t_f2r)
        bld.op0("vector", lambda e: e.tensor_copy(fidx1_u[:, :], fidx1_s[:, :]))
        bld.op0("vector", lambda e: e.tensor_copy(fidx2_u[:, :], fidx2_s[:, :]))
        mx3 = masked_s[:, :].rearrange("p (i e) -> p i e", e=E)
        bld.op0("vector", lambda e: e.tensor_tensor(
            mx3, sq3, ecapb_s[:, :].unsqueeze(1).broadcast_to([128, NT, E]),
            ALU.add))
        bld.op0("vector", lambda e: e.tensor_scalar(
            masked_s[:, :], masked_s[:, :], BIG + 1.0, None, ALU.subtract))
        bld.op0("vector", lambda e: e.tensor_tensor(
            masked_s[:, :], valid_s[:, :], masked_s[:, :], ALU.mult))
        bld.op0("vector", lambda e: e.tensor_scalar(
            masked_s[:, :], masked_s[:, :], BIG, None, ALU.add))
        bld.op0("vector", lambda e: e.tensor_tensor(ff3, mx3, mk1, ALU.mult))
        bld.op0("vector", lambda e: e.tensor_reduce(
            k1_s[:, :], ff3, axis=AXX, op=ALU.add))
        bld.op0("vector", lambda e: e.tensor_tensor(ff3, mx3, mk2, ALU.mult))
        t_k2r = bld.op("vector", lambda e: e.tensor_reduce(
            k2_s[:, :], ff3, axis=AXX, op=ALU.add))
        bld.wait("vector", t_k2r)
        bld.op0("vector", lambda e: e.tensor_copy(k1_u[:, :], k1_s[:, :]))
        t_fu = bld.op("vector", lambda e: e.tensor_copy(k2_u[:, :], k2_s[:, :]))
        bld.waits("gpsimd", [tk["zeros"]])
        t_zero = bld.dma("gpsimd", "qZ",
                         slot_loc.ap().rearrange("r c -> (r c)")
                         .rearrange("(p f) -> p f", p=128),
                         zeros_s[:, :])
        bld.waits("gpsimd", [t_fu, t_zero])
        slot_flat = slot_loc.ap().rearrange("r c -> (r c)").unsqueeze(1)
        for i in range(NT):
            for fu in (k1_u, k2_u):
                t_sc = bld.idma("qSC", slot_flat,
                                IndirectOffsetOnAxis(ap=fu[:, i:i + 1], axis=0),
                                tokids_s[:, i:i + 1], None, E * CAP - 1)
        bld.wait("gpsimd", t_sc)
        t_rsc = bld.collective("ReduceScatter", ALU.add,
                               [slot_loc.ap().opt()], [slot_sh.ap().opt()],
                               f"ccR{ph}", G8)
        bld.waits("gpsimd", [tk[(ph, "lnst")]])
        t_agl.append(bld.collective(
            "AllGather", ALU.bypass, [lnx_loc.ap()[3 * NQ:N, :].opt()],
            [ag_lnx.ap()[NCORES * 3 * NQ:TT, :].opt()], f"ccL{ph}d", G8))
        t_ag_lnx = list(t_agl)
        bld.wait("gpsimd", t_rsc)
        t_shl = bld.dma("gpsimd", "qSX", slotsh_s[:, :], slot_sh.ap())
        bld.waits("tensor", [t_shl, t_logev])
        t_shT = bld.op("tensor", lambda e: e.transpose(
            psGC[:, 32:32 + GPC], slotsh_s[:, :], idf_s[0:GPC, 0:GPC]))
        bld.wait("vector", t_shT)
        bld.op0("vector", lambda e: e.tensor_copy(sidxf_s[:, :],
                                                  psGC[:, 32:32 + GPC]))
        bld.op0("vector", lambda e: e.tensor_scalar(
            sidxf_s[:, :], sidxf_s[:, :], 1.0, 0.0, ALU.subtract, ALU.max))
        t_sx = bld.op("vector", lambda e: e.tensor_copy(sidx_u[:, :],
                                                        sidxf_s[:, :]))

        pexp = ExitStack()
        grows = [SB(pexp, f"grow{ph}_{i}", [128, D], BF16) for i in range(GPT)]
        dispT = [SB(pexp, f"dispT{ph}_{i}", [128, DT * SCH], BF16)
                 for i in range(2)]
        h_sb = SB(pexp, f"h_sb{ph}", [128, HT * SCH], BF16)
        w1s = [SB(pexp, f"w1s{ph}_{i}", [128, DT * HB], BF16) for i in range(2)]
        w2s = [SB(pexp, f"w2s{ph}_{i}", [128, DT * HB], BF16) for i in range(2)]
        NW3B = 8
        w3s = [SB(pexp, f"w3s{ph}_{i}", [128, D], BF16) for i in range(NW3B)]
        s1_b = [SB(pexp, f"s1b{ph}_{i}", [128, SCH], BF16) for i in range(2)]
        outs_b = [SB(pexp, f"outs{ph}_{i}", [128, D], BF16) for i in range(2)]

        def emit_gathers(s):
            bld.waits("gpsimd", [t_sx, t_ag_lnx, tk.get((ph, "dtev", s - 1))])
            for g in range(GPT):
                t_ga = bld.idma(
                    f"qD{s % 2}",
                    grows[g][:, :], None, ag_lnx.ap(),
                    IndirectOffsetOnAxis(
                        ap=sidx_u[:, s * GPT + g:s * GPT + g + 1], axis=0),
                    TT - 1)
            tk[(ph, "tga", s)] = t_ga

        def emit_disp_tr(s):
            par = s % 2
            bld.waits("tensor", [tk[(ph, "tga", s)],
                                 tk.get((ph, "hdone", s - 2))])
            for g in range(GPT):
                bld.waits("tensor", [tk.get("psT_free")])
                for a in range(DT):
                    t_tp = bld.op("tensor", lambda e, g=g, a=a: e.transpose(
                        psT[:, a * 128:(a + 1) * 128],
                        grows[g][:, a * 128:(a + 1) * 128], idb_s[:, :]))
                bld.wait("scalar", t_tp)
                t_dev = bld.op("scalar", lambda e, g=g, par=par: e.copy(
                    dispT[par][:, :].rearrange("p (a t) -> p a t", a=DT)
                    [:, :, g * 128:(g + 1) * 128], psT[:, :]))
                tk["psT_free"] = t_dev
            tk[(ph, "dtev", s)] = t_dev

        if ph == 1:
            emit_precast("A")
        bld.waits("gpsimd", [tk.get("att_done")])
        bld.waits("sync", [tk.get("att_done")])
        emit_gathers(0)
        emit_disp_tr(0)
        if NSC > 1:
            emit_gathers(1)
        if ph == 1:
            emit_precast("B")
            emit_precast("C")

        t_ago1 = None
        for s in range(NSC):
            par = s % 2
            if s >= 1:
                emit_disp_tr(s)
                if s + 1 <= NSC - 1:
                    emit_gathers(s + 1)
            if s == S_HALF and s > 0:
                nlw = tk[(ph, "lin_w")]
                bld.waits("gpsimd", [tk.get((ph, "ost", nlw - 1)),
                                     tk.get((ph, "ost", nlw - 2))])
                t_ago1 = bld.collective(
                    "AllGather", ALU.bypass, [out_loc.ap()[0:HCAP, :].opt()],
                    [out_all.ap()[0:E * HCAP, :].opt()], f"ccO{ph}a", G8)
            t_dev = tk[(ph, "dtev", s)]
            for hb in range(NHB):
                wpar = hb % 2
                lin_hb = s * NHB + hb
                bld.waits("sync", [tk.get(("wfc", (hb * HB) // STGW)),
                                   tk.get((ph, "wcons", lin_hb - 2))])
                t_w1 = bld.dma("sync", f"qW{wpar}", w1s[wpar][:, :],
                               w1b.ap().rearrange("(a p) h -> p a h", p=128)
                               [:, :, hb * HB:(hb + 1) * HB])
                t_w2 = bld.dma("sync", f"qW{wpar}", w2s[wpar][:, :],
                               w2b.ap().rearrange("(a p) h -> p a h", p=128)
                               [:, :, hb * HB:(hb + 1) * HB])
                for j in range(JT):
                    jj = hb * JT + j
                    for (v0, sv) in SV:
                        pp = (jj * len(SV) + SV.index((v0, sv))) % 2
                        bld.waits("tensor", [t_dev, t_w1, t_w2,
                                             tk.get((ph, "h1f", pp))])
                        for a in range(DT):
                            t_h1 = bld.op("tensor", lambda e, a=a, j=j, v0=v0,
                                          sv=sv, par=par, wpar=wpar, pp=pp:
                                          e.matmul(
                                pool[pp][:, 0:sv],
                                w1s[wpar][:, a * HB + j * 128:
                                          a * HB + (j + 1) * 128],
                                dispT[par][:, a * SCH + v0:a * SCH + v0 + sv],
                                start=(a == 0), stop=(a == DT - 1)))
                        bld.waits("scalar", [t_h1, tk.get((ph, "s1f", pp))])
                        t_s1 = bld.op("scalar", lambda e, sv=sv, pp=pp:
                                      e.activation(s1_b[pp][:, 0:sv],
                                                   pool[pp][:, 0:sv], AF.Silu))
                        tk[(ph, "h1f", pp)] = t_s1
                        tk[(ph, "siluf")] = t_s1
                        bld.waits("tensor", [tk.get((ph, "h2f", pp))])
                        for a in range(DT):
                            t_h2 = bld.op("tensor", lambda e, a=a, j=j, v0=v0,
                                          sv=sv, par=par, wpar=wpar, pp=pp:
                                          e.matmul(
                                pool[2 + pp][:, 0:sv],
                                w2s[wpar][:, a * HB + j * 128:
                                          a * HB + (j + 1) * 128],
                                dispT[par][:, a * SCH + v0:a * SCH + v0 + sv],
                                start=(a == 0), stop=(a == DT - 1)))
                        bld.waits("vector", [t_s1, t_h2])
                        t_h = bld.op("vector", lambda e, jj=jj, v0=v0, sv=sv,
                                     pp=pp: e.tensor_tensor(
                            h_sb[:, jj * SCH + v0:jj * SCH + v0 + sv],
                            s1_b[pp][:, 0:sv], pool[2 + pp][:, 0:sv],
                            ALU.mult))
                        tk[(ph, "s1f", pp)] = t_h
                        tk[(ph, "h2f", pp)] = t_h
                        tk[(ph, "multf")] = t_h
                tk[(ph, "wcons", lin_hb)] = [t_h1, t_h2]
            tk[(ph, "hdone", s)] = t_h
            for vidx, (v0, sv) in enumerate(SV):
                ntile = sv // 128
                for jj in range(HT):
                    w3par = jj % NW3B
                    lin_jj = (s * len(SV) + vidx) * HT + jj
                    bld.waits("sync", [tk.get((ph, "w3cons", lin_jj - NW3B)),
                                       tk.get(("w3r", jj))])
                    t_w3 = bld.dma("sync", f"qT{w3par}", w3s[w3par][:, :],
                                   w3b.ap()[jj * 128:(jj + 1) * 128, :])
                    bld.waits("tensor", [t_w3, tk[(ph, "hdone", s)]])
                    if jj == 0:
                        bld.waits("tensor", [tk.get((ph, "siluf")),
                                             tk.get((ph, "multf")),
                                             tk.get((ph, "oevlast"))])
                    for w in range(ntile):
                        for dv in range(NDV):
                            t_o = bld.op("tensor", lambda e, jj=jj, w=w, dv=dv,
                                         v0=v0, w3par=w3par: e.matmul(
                                pool[w * NDV + dv][:, 0:DHS],
                                h_sb[:, jj * SCH + v0 + w * 128:
                                     jj * SCH + v0 + (w + 1) * 128],
                                w3s[w3par][:, dv * DHS:(dv + 1) * DHS],
                                start=(jj == 0), stop=(jj == HT - 1)))
                    tk[(ph, "w3cons", lin_jj)] = t_o
                tk[(ph, "w3mm", s)] = t_o
                bld.wait("scalar", t_o)
                for w in range(ntile):
                    lin_w = tk.get((ph, "lin_w"), 0)
                    tk[(ph, "lin_w")] = lin_w + 1
                    opar = lin_w % 2
                    bld.waits("scalar", [tk.get((ph, "ost", lin_w - 2))])
                    for dv in range(NDV):
                        t_oe = bld.op("scalar", lambda e, w=w, dv=dv, opar=opar:
                                      e.copy(outs_b[opar][:, dv * DHS:
                                                          (dv + 1) * DHS],
                                             pool[w * NDV + dv][:, 0:DHS]))
                    tk[(ph, "oevlast")] = t_oe
                    bld.wait("sync", t_oe)
                    r0 = s * SCH + v0 + w * 128
                    t_ost = bld.dma("sync", f"qO{opar}", out_loc.ap()[r0:r0 + 128, :],
                                    outs_b[opar][:, :])
                    tk[(ph, "ost", lin_w)] = t_ost
        nlw = tk[(ph, "lin_w")]
        bld.waits("gpsimd", [tk.get((ph, "ost", nlw - 1)),
                             tk.get((ph, "ost", nlw - 2))])
        if t_ago1 is None:
            t_ago1 = bld.collective(
                "AllGather", ALU.bypass, [out_loc.ap()[0:HCAP, :].opt()],
                [out_all.ap()[0:E * HCAP, :].opt()], f"ccO{ph}a", G8)
        bld.waits("gpsimd", [tk["zrow"]])
        t_ago2 = bld.collective(
            "AllGather", ALU.bypass, [out_loc.ap()[HCAP:CAP + 1, :].opt()],
            [out_all.ap()[E * HCAP:E * (CAP + 1), :].opt()], f"ccO{ph}b", G8)
        t_ago = [t_ago1, t_ago2]
        pexp.close()
        nlw2 = tk[(ph, "lin_w")]
        tk[(ph, "expfree")] = [tk.get((ph, "ost", nlw2 - 1)),
                               tk.get((ph, "ost", nlw2 - 2)),
                               tk.get((ph, "oevlast")),
                               tk[(ph, "hdone", NSC - 1)]]
        r12_b = [SB(pes, f"r12b{ph}_{i}", [128, 2 * D], BF16) for i in range(4)]
        xt2_b = [SB(pes, f"xt2{ph}_{i}", [128, D]) for i in range(4)]
        y0_b = [SB(pes, f"y0b{ph}_{i}", [128, D]) for i in range(4)]
        t_y = None
        for i in range(NT):
            p4 = i % 4
            par = i % 4
            bld.waits("vector", [tk.get((ph, "ycons", i - 4)), t_wh2])
            bld.waits("gpsimd", [tk.get((ph, "rfree", i - 4)), t_ago])
            bld.idma(f"qC{p4}", r12_b[p4][:, 0:D], None, out_all.ap(),
                     IndirectOffsetOnAxis(ap=fidx1_u[:, i:i + 1], axis=0),
                     E * (CAP + 1) - 1)
            t_gd = bld.idma(f"qC{p4}", r12_b[p4][:, D:2 * D], None, out_all.ap(),
                            IndirectOffsetOnAxis(ap=fidx2_u[:, i:i + 1], axis=0),
                            E * (CAP + 1) - 1)
            bld.waits("scalar", [tk.get((ph, "ycons", i - 4))])
            t_x = bld.dma("scalar", f"qCX{par}", xt2_b[par][:, :],
                          x_ap[i * 128:(i + 1) * 128, :])
            bld.waits("vector", [t_gd, t_x])
            bld.op0("vector", lambda e, p4=p4, par=par, i=i:
                    e.scalar_tensor_tensor(
                y0_b[par][:, :], r12_b[p4][:, 0:D], w1h_s[:, i:i + 1],
                xt2_b[par][:, :], ALU.mult, ALU.add))
            t_y = bld.op("vector", lambda e, p4=p4, par=par, i=i:
                         e.scalar_tensor_tensor(
                xt2_b[par][:, :], r12_b[p4][:, D:2 * D], w2h_s[:, i:i + 1],
                y0_b[par][:, :], ALU.mult, ALU.add))
            tk[(ph, "rfree", i)] = t_y
            bld.wait("sync", t_y)
            t_yst = bld.dma("sync", f"qY{par}", y_ap[i * 128:(i + 1) * 128, :],
                            xt2_b[par][:, :])
            tk[(ph, "ycons", i)] = [t_y, t_yst]
        if not keep_scope:
            pes.close()
        tk[(ph, "pes")] = pes
        return [tk.get((ph, "ycons", NT - 1 - k)) for k in range(4)]

    def emit_attn():
        yc_end = [tk.get((1, "ycons", NT - 1 - k)) for k in range(4)]
        expfree = tk.get((1, "expfree"))
        aes = ExitStack()
        winT_s = SB(aes, "winT_s", [128, DT * D], BF16)
        walT_s = SB(aes, "walT_s", [128, DT * D], BF16)
        wbeT_s = SB(aes, "wbeT_s", [128, DT * D], BF16)
        wvT_s = SB(aes, "wvT_s", [128, DT * D], BF16)
        woT_s = walT_s
        NLT = 2
        ln2T = [SB(aes, f"ln2T_{i}", [128, DT * TCH], BF16) for i in range(NLT)]
        xpT = [SB(aes, f"xpT_{i}", [128, DT * TCH], BF16) for i in range(2)]
        aln_b = [SB(aes, f"aln{i}", [128, D], BF16) for i in range(4)]

        bld.waits("gpsimd", [expfree])
        for (src, dst) in [(winT, winT_s), (walT, walT_s), (wbeT, wbeT_s),
                           (wvT, wvT_s)]:
            for a in range(DT):
                tk["attw"] = bld.dma("gpsimd", "qAW", dst[:, a * D:(a + 1) * D],
                                     src.ap()[a * 128:(a + 1) * 128, :])

        tk[(1, "pes")].close()
        NAB = 3
        al_b = [SB(aes, f"al_b{i}", [128, TCH]) for i in range(NAB)]
        be_b = [SB(aes, f"be_b{i}", [128, TCH]) for i in range(NAB)]
        vv_b = [SB(aes, f"vv_b{i}", [128, TCH]) for i in range(NAB)]
        bvb = [SB(aes, f"bvb{i}", [128, TCH]) for i in range(NAB)]
        hscr = [SB(aes, f"hscr{i}", [128, TCH]) for i in range(2)]
        silb = [SB(aes, f"silb{i}", [128, TCH]) for i in range(2)]
        hcarry = SB(aes, "hcarry", [128, FT])
        hcarry2 = hcarry
        g_b = SB(aes, "g_b", [128, FT * TCH], BF16)
        NRL = 3
        al2 = [SB(aes, f"al2_{i}", [128, TCH]) for i in range(NRL)]
        bv2 = [SB(aes, f"bv2_{i}", [128, TCH]) for i in range(NRL)]
        xo_b = [SB(aes, f"xo_{i}", [128, D]) for i in range(2)]
        xr_b = [SB(aes, f"xr_{i}", [128, D]) for i in range(2)]
        hend_s = SB(aes, "hend_s", [128, FT])
        bld.waits("vector", [yc_end])
        t_hc0 = bld.op("vector", lambda e: e.memset(hcarry[:, :], 0.0))
        scan_t = t_hc0
        spill_t = []
        TPC = TCH // 128

        for j in range(TPC):
            r = emit_ln_stats(j, xr1.ap(), aln_b, 4,
                              dep=tk.get((1, "ycons", j)), alt=True)
            tk[("a1lnst", j)] = r["ln"]
        for ch in range(NCH):
            par = ch % NLT
            xpar = ch % 2
            bld.waits("scalar", [tk.get(("a1xpev", ch - NLT))])
            if ch == 0:
                bld.waits("tensor", [tk["attw"]])
            for j in range(TPC):
                i = ch * TPC + j
                t_l2 = emit_ln_tr(
                    i, tk[("a1lnst", i)], aln_b, 4,
                    ln2T[par][:, :].rearrange("p (a t) -> p a t", a=DT)
                    [:, :, j * 128:(j + 1) * 128])
                tk[("ln_free", i)] = [t_l2]
            if ch + 1 < NCH:
                for j in range(TPC):
                    i = (ch + 1) * TPC + j
                    r = emit_ln_stats(i, xr1.ap(), aln_b, 4,
                                      dep=tk.get((1, "ycons", i)), alt=True)
                    tk[("a1lnst", i)] = r["ln"]
            for b in range(DT):
                pp = b % 2
                bld.waits("tensor", [t_l2, tk.get(("a1xpf", pp))])
                for a in range(DT):
                    t_xp = bld.op("tensor", lambda e, a=a, b=b, par=par, pp=pp:
                                  e.matmul(
                        pool[pp][:, 0:TCH],
                        winT_s[:, a * D + b * 128:a * D + (b + 1) * 128],
                        ln2T[par][:, a * TCH:(a + 1) * TCH],
                        start=(a == 0), stop=(a == DT - 1)))
                bld.wait("scalar", t_xp)
                t_xe = bld.op("scalar", lambda e, b=b, xpar=xpar, pp=pp:
                              e.activation(xpT[xpar][:, b * TCH:(b + 1) * TCH],
                                           pool[pp][:, 0:TCH], AF.Silu))
                tk[("a1xpf", pp)] = t_xe
            tk[("a1xpev", ch)] = t_xe
            if ch == 0:
                bld.waits("scalar", [yc_end])
                bld.waits("gpsimd", [yc_end])
            for b in range(FT):
                pp = (ch * FT + b) % NAB
                trips = []
                for (wsb, bias, dstl, fn) in ((walT_s, bal_s, al_b, AF.Sigmoid),
                                              (wbeT_s, bbe_s, be_b, AF.Sigmoid),
                                              (wvT_s, bvv_s, vv_b, AF.Tanh)):
                    agc = tk.get("a1agc", 0)
                    tk["a1agc"] = agc + 1
                    bk = 2 + (agc % 4)
                    bld.waits("tensor", [t_xe, tk.get(("a1bank", bk))])
                    for a in range(DT):
                        t_gm = bld.op("tensor", lambda e, a=a, b=b, xpar=xpar,
                                      bk=bk, wsb=wsb: e.matmul(
                            pool[bk][:, 0:TCH],
                            wsb[:, a * D + b * 128:a * D + (b + 1) * 128],
                            xpT[xpar][:, a * TCH:(a + 1) * TCH],
                            start=(a == 0), stop=(a == DT - 1)))
                    bld.waits("scalar", [t_gm, tk.get(("a1ev", id(dstl), pp))])
                    t_ge = bld.op("scalar", lambda e, b=b, dstl=dstl, fn=fn,
                                  pp=pp, bk=bk, bias=bias: e.activation(
                        dstl[pp][:, :], pool[bk][:, 0:TCH], fn,
                        bias=bias[:, b:b + 1]))
                    tk[("a1bank", bk)] = t_ge
                    tk[("a1gf", id(dstl), pp)] = t_ge
                    if wsb is walT_s:
                        tk[("a1wal_last")] = t_ge
                    trips.append(t_ge)
                bld.waits("vector", [trips[1], trips[2],
                                     tk.get(("a1bvf", pp))])
                t_bv = bld.op("vector", lambda e, pp=pp: e.tensor_tensor(
                    bvb[pp][:, :], be_b[pp][:, :], vv_b[pp][:, :], ALU.mult))
                tk[("a1ev", id(be_b), pp)] = t_bv
                tk[("a1ev", id(vv_b), pp)] = t_bv
                tk[("a1bvt", b)] = [t_bv, trips[0], pp]

                def p1_scan(bb):
                    t_bvb, t_alb, ppb = tk[("a1bvt", bb)]
                    hrb = bb % 2
                    bld.waits("vector", [t_alb, tk.get(("p1c", bb))])
                    t_s = bld.op("vector", lambda e, ppb=ppb, bb=bb, hrb=hrb:
                                 e.tensor_tensor_scan(
                        hscr[hrb][:, :], al_b[ppb][:, :], bvb[ppb][:, :],
                        hcarry[:, bb:bb + 1], ALU.mult, ALU.add))
                    t_c = bld.op("vector", lambda e, bb=bb, hrb=hrb:
                                 e.tensor_copy(hcarry[:, bb:bb + 1],
                                               hscr[hrb][:, TCH - 1:TCH]))
                    tk[("p1c", bb)] = t_c
                    bld.waits("sync", [t_alb, t_c])
                    chb = tk[("a1ch", bb)]
                    t_sp1 = bld.dma("sync", f"qSP{ppb}",
                                    alpha_sp.ap()[bb * 128:(bb + 1) * 128,
                                                  chb * TCH:(chb + 1) * TCH],
                                    al_b[ppb][:, :])
                    t_sp2 = bld.dma("sync", f"qSP{ppb}",
                                    bvv_sp.ap()[bb * 128:(bb + 1) * 128,
                                                chb * TCH:(chb + 1) * TCH],
                                    bvb[ppb][:, :])
                    tk[("a1ev", id(al_b), ppb)] = [t_sp2, t_c]
                    tk[("a1bvf", ppb)] = [t_sp2, t_c]
                    tk[("spill_last", ppb)] = t_sp2

                tk[("a1ch", b)] = ch
                if b > 0:
                    p1_scan(b - 1)
                if b == FT - 1:
                    p1_scan(b)
        spill_t = [tk.get(("spill_last", i)) for i in range(NAB)]

        bld.waits("gpsimd", [tk[("a1wal_last")]])
        for a in range(DT):
            tk["wout"] = bld.dma("gpsimd", "qAW", woT_s[:, a * D:(a + 1) * D],
                                 woT.ap()[a * 128:(a + 1) * 128, :])
        bld.waits("sync", [[tk.get(("p1c", b)) for b in range(FT)]] + spill_t)
        t_he = bld.dma("sync", "qHE", hend_loc.ap(), hcarry[:, :])
        bld.wait("gpsimd", t_he)
        t_agh = bld.collective("AllGather", ALU.bypass,
                               [hend_loc.ap().opt()], [hend_all.ap().opt()],
                               "ccH", GP)
        bld.wait("gpsimd", t_agh)
        t_hl = bld.dma("gpsimd", "qHL", hend_s[:, :], hend_all.ap()[0:128, :])
        bld.waits("vector", [t_hl, [tk.get(("p1c", b)) for b in range(FT)]])
        t_hy = bld.op("vector", lambda e: e.tensor_tensor(
            hcarry[:, :], hend_s[:, :],
            isodd_s[:, 0:1].broadcast_to([128, FT]), ALU.mult))
        for b in range(FT):
            tk[("p2c", b)] = t_hy

        bld.waits("scalar", [spill_t])
        t_st = None
        for ch in range(NCH):
            gdone_t = None
            for b in range(FT):
                pp = b % NRL
                lin_b = ch * FT + b
                bld.waits("sync", [tk.get(("a2rel", lin_b - NRL))])
                ra = bld.dma("sync", f"qRL{pp}", al2[pp][:, :],
                             alpha_sp.ap()[b * 128:(b + 1) * 128,
                                           ch * TCH:(ch + 1) * TCH])
                rb = bld.dma("sync", f"qRL{pp}", bv2[pp][:, :],
                             bvv_sp.ap()[b * 128:(b + 1) * 128,
                                         ch * TCH:(ch + 1) * TCH])
                hr = b % 2
                sr = b % 2
                bld.waits("vector", [ra, rb, tk.get(("p2c", b))])
                t_s = bld.op("vector", lambda e, pp=pp, b=b, hr=hr:
                             e.tensor_tensor_scan(
                    hscr[hr][:, :], al2[pp][:, :], bv2[pp][:, :],
                    hcarry2[:, b:b + 1], ALU.mult, ALU.add))
                scan2_t = bld.op("vector", lambda e, b=b, hr=hr: e.tensor_copy(
                    hcarry2[:, b:b + 1], hscr[hr][:, TCH - 1:TCH]))
                tk[("p2c", b)] = scan2_t
                tk[("a2rel", lin_b)] = scan2_t
                bld.waits("scalar", [t_s, tk.get(("a2silR", sr))])
                t_sil = bld.op("scalar", lambda e, hr=hr, sr=sr: e.activation(
                    silb[sr][:, :], hscr[hr][:, :], AF.Silu))
                tk[("a2silt", b)] = (t_sil, hr, sr)

                def p2_g(bb):
                    t_silb, hrb, srb = tk[("a2silt", bb)]
                    bld.waits("vector", [t_silb, tk.get(("a2oplast", ch - 1))])
                    g_t = bld.op("vector", lambda e, bb=bb, hrb=hrb, srb=srb:
                                 e.tensor_tensor(
                        g_b[:, bb * TCH:(bb + 1) * TCH], hscr[hrb][:, :],
                        silb[srb][:, :], ALU.mult))
                    tk[("a2silR", srb)] = g_t
                    return g_t

                if b > 0:
                    gdone_t = p2_g(b - 1)
                if b == FT - 1:
                    gdone_t = p2_g(b)
            tk[("a2gb", ch)] = gdone_t
            for j in range(TCH // 128):
                i = ch * (TCH // 128) + j
                par = i % 2
                bld.waits("scalar", [tk.get(("a2yc", i - 2))])
                t_xr = bld.dma("scalar", f"qXR{par}", xr_b[par][:, :],
                               xr1.ap()[i * 128:(i + 1) * 128, :])
                bld.waits("tensor", [gdone_t, tk.get(("a2of", i - 2)),
                                     tk["wout"]])
                for dv in range(NDV):
                    for a in range(FT):
                        t_op = bld.op("tensor", lambda e, a=a, j=j, dv=dv,
                                      par=par: e.matmul(
                            pool[par * NDV + dv][:, 0:DHS],
                            g_b[:, a * TCH + j * 128:a * TCH + (j + 1) * 128],
                            woT_s[:, a * D + dv * DHS:a * D + (dv + 1) * DHS],
                            start=(a == 0), stop=(a == FT - 1)))
                bld.waits("vector", [t_op, t_xr, tk.get(("a2yc", i - 2))])
                for dv in range(NDV):
                    t_yv = bld.op("vector", lambda e, par=par, dv=dv:
                                  e.tensor_tensor(
                        xo_b[par][:, dv * DHS:(dv + 1) * DHS],
                        pool[par * NDV + dv][:, 0:DHS],
                        xr_b[par][:, dv * DHS:(dv + 1) * DHS], ALU.add))
                tk[("a2of", i)] = t_yv
                tk[("a2oplast", ch)] = t_op
                bld.wait("sync", t_yv)
                t_st = bld.dma("sync", f"qY{par}", xr2.ap()[i * 128:(i + 1) * 128, :],
                               xo_b[par][:, :])
                tk[("a2yc", i)] = [t_yv, t_st]
        aes.close()
        tk["att_done"] = [tk[("a2yc", N // 128 - 1)],
                          tk.get(("a2yc", N // 128 - 2)),
                          tk.get(("a2of", N // 128 - 1)),
                          tk.get(("a2silR", 0)), tk.get(("a2silR", 1))]
        return [tk[("a2yc", N // 128 - 1)], tk.get(("a2yc", N // 128 - 2))]

    emit_setup()
    if phases == "1A2":
        t_f1 = emit_ffn(1, x_in.ap(), xr1.ap())
        bld.waits("sync", [t_f1])
        bld.barrier()
        t_at = emit_attn()
        bld.waits("sync", [t_at])
        bld.barrier()
        t_f2 = emit_ffn(2, xr2.ap(), y_out.ap())
        bld.waits("gpsimd", [t_f2])
        bld.waits("sync", [t_f2])
        bld.barrier()
    elif phases == "1":
        t_f1 = emit_ffn(1, x_in.ap(), y_out.ap())
        bld.waits("gpsimd", [t_f1])
        bld.waits("sync", [t_f1])
        bld.barrier()
    elif phases == "1A":
        t_f1 = emit_ffn(1, x_in.ap(), xr1.ap())
        bld.waits("sync", [t_f1])
        bld.barrier()
        t_at = emit_attn()
        bld.waits("gpsimd", [t_at])
        bld.waits("sync", [t_at])
        bld.barrier()

    with nc.Block() as blk:
        for eng in ENGS:
            fns = bld.q[eng]
            if fns:
                def body(e, fns=fns):
                    for f in fns:
                        f(e)
                getattr(blk, eng)(body)
    root.close()
    return nc, c



def make_inputs_per_core(c, full):
    D, E, CAP, N, NT, FT = c["D"], c["E"], c["CAP"], c["N"], c["NT"], c["DT"]
    HC2 = (3 * CAP // 4) // 128 * 128
    x = np.ascontiguousarray(
        np.asarray(full["L0_attn_x"]).astype(np.float32, copy=False)
        .reshape(c["TT"], D))
    f32 = lambda k: np.asarray(full[k]).astype(np.float32, copy=False)
    w_in, w_al, w_be, w_v, w_o = (f32("L0_attn_W_in"), f32("L0_attn_W_alpha"),
                                  f32("L0_attn_W_beta"), f32("L0_attn_W_v"),
                                  f32("L0_attn_W_out"))
    b_al, b_be, b_v = (f32("L0_attn_b_alpha"), f32("L0_attn_b_beta"),
                       f32("L0_attn_b_v"))
    wg, bg = f32("L0_ffn_Wg"), f32("L0_ffn_bg")
    w1, w2, w3 = f32("L0_ffn_W1"), f32("L0_ffn_W2"), f32("L0_ffn_W3")

    bf = ml_dtypes.bfloat16
    lay_bias = lambda b: np.ascontiguousarray(b.reshape(FT, 128).T)
    common = {
        "winT": np.ascontiguousarray(w_in.T),
        "walT": np.ascontiguousarray(w_al.T),
        "wbeT": np.ascontiguousarray(w_be.T),
        "wvT": np.ascontiguousarray(w_v.T),
        "woT": np.ascontiguousarray(w_o.T),
        "b_al": lay_bias(b_al), "b_be": lay_bias(b_be), "b_vv": lay_bias(b_v),
        "wg": np.ascontiguousarray(wg), "bg": np.ascontiguousarray(bg.reshape(1, E)),
        "idb": np.eye(128, dtype=bf),
        "idf": np.eye(128, dtype=np.float32),
        "u128": np.triu(np.ones((128, 128), np.float32)).astype(bf),
        "ones1": np.ones((1, 128), np.float32),
        "onescol": np.ones((128, 1), np.float32).astype(bf),
        "ecap8": np.ascontiguousarray(np.broadcast_to(
            (np.arange(E) * HC2).astype(np.float32), (128, E))),
        "ecapb": np.ascontiguousarray(np.broadcast_to(
            (np.arange(E) * CAP).astype(np.float32), (128, E))),
        "eoff8": np.ascontiguousarray(np.broadcast_to(
            ((E - 1) * HC2 + np.arange(E) * (CAP + 1 - 2 * HC2))
            .astype(np.float32), (128, E))),
    }
    maps = []
    for cid in range(NCORES):
        sel = np.zeros((E, 1), np.float32)
        sel[:cid] = 1.0
        l = np.arange(N, dtype=np.int64)
        NQ = N // 4
        row = (l // NQ) * (NCORES * NQ) + cid * NQ + (l % NQ)
        tok = np.ascontiguousarray(
            (row.astype(np.float32) + 1.0).reshape(NT, 128).T)
        m = dict(common)
        m.update({
            "x": np.ascontiguousarray(x[cid * N:(cid + 1) * N]),
            "w1": np.ascontiguousarray(w1[cid]),
            "w2": np.ascontiguousarray(w2[cid]),
            "w3": np.ascontiguousarray(w3[cid]),
            "selc": sel,
            "isodd": np.full((128, 1), float(cid % 2), np.float32),
            "tokids": tok,
        })
        maps.append(m)
    return maps


_CACHE = {}


def kernel(**inputs):
    c = derive(FULL)
    if "nc" not in _CACHE:
        _CACHE["nc"] = build_program(FULL)[0]
    nc = _CACHE["nc"]
    maps = make_inputs_per_core(c, inputs)
    res = run_bass_kernel_spmd(nc, maps, core_ids=list(range(NCORES)))
    out = np.concatenate([np.asarray(res.results[i]["y"]).astype(np.float32)
                          for i in range(NCORES)], axis=0)
    return out.reshape(c["B"], c["TSEQ"], c["D"])
